# version 08
# baseline (speedup 1.0000x reference)
import numpy as np
from contextlib import ExitStack

import concourse.bass as bass
import concourse.mybir as mybir
from concourse.bass_utils import run_bass_kernel_spmd

F32 = mybir.dt.float32
BF16 = mybir.dt.bfloat16
AF = mybir.ActivationFunctionType
ALU = mybir.AluOpType
AX = mybir.AxisListType

BC = 2
L = 2048
DM = 1024
NH = 16
HD = 64
NP = 8
NWIN = 4
NT = 16
NFC = 8
GRP = 4


class Em:

    ENGS = ("t", "v", "s", "g", "y")
    OWN = {"t": "st", "v": "sv", "s": "ss"}

    def __init__(self, sems):
        self.L = {k: [] for k in self.ENGS}
        self.sems = sems
        self.c = {k: 0 for k in sems}
        self.lastwait = {}

    def val(self, name):
        return self.c[name]

    def bump(self, name, n=1):
        self.c[name] += n
        return self.c[name]

    def wait_val(self, eng, name, v):
        if v <= 0:
            return
        if self.OWN.get(eng) == name:
            return
        key = (eng, name)
        if self.lastwait.get(key, 0) >= v:
            return
        self.lastwait[key] = v
        sem = self.sems[name]
        self.L[eng].append(lambda e, sem=sem, v=v: e.wait_ge(sem, v))

    def wait(self, eng, name):
        self.wait_val(eng, name, self.val(name))

    def op(self, eng, fn, inc=None, n=1):
        if eng == "v":
            own = self.OWN[eng]
            pv = self.val(own)
            sem0 = self.sems[own]
            if pv > 0:
                self.L[eng].append(lambda e, sem0=sem0, pv=pv: e.wait_ge(sem0, pv))
            inc = own
            n = 1
        if inc is None:
            self.L[eng].append(fn)
            return None
        v = self.bump(inc, n)
        sem = self.sems[inc]
        self.L[eng].append(lambda e, fn=fn, sem=sem, n=n: fn(e).then_inc(sem, n))
        return v

    def self_sync(self, eng):
        name = self.OWN[eng]
        v = self.val(name)
        sem = self.sems[name]
        self.L[eng].append(lambda e, sem=sem, v=v: e.wait_ge(sem, v))

    def barrier(self, engs=("t", "v", "s")):
        vals = {k: self.val(v) for k, v in self.OWN.items()}
        for eng in engs:
            for src, name in self.OWN.items():
                if src != eng:
                    self.wait_val(eng, name, vals[src])


def build_program(phases=99.0):
    nc = bass.Bass("TRN2", target_bir_lowering=False, detect_race_conditions=False)

    def din(name, shape):
        return nc.dram_tensor(name, list(shape), F32, kind="ExternalInput").ap()

    x_d = din("x", (BC, L, DM))
    wqp_d = din("wqp", (NP, 128, DM))
    wqs_d = din("wqs", (NP, 128, DM))
    wkp_d = din("wkp", (NP, 128, DM))
    wks_d = din("wks", (NP, 128, DM))
    wvn_d = din("wvn", (NP, 128, DM))
    wex_d = din("wex", (9, 2, 128, NFC * 512))
    gw_d = din("gw", (128, 64))
    bqp_d = din("bqp", (128, 8))
    bqs_d = din("bqs", (128, 8))
    bkp_d = din("bkp", (128, 8))
    bks_d = din("bks", (128, 8))
    bvn_d = din("bvn", (128, 8))
    bo_d = din("bo", (1, DM))
    gb_d = din("gb", (1, 8))
    eb_d = din("eb", (8, DM))
    ct_d = din("ct", (128, L))
    st_d = din("stb", (128, L))
    tri_d = din("tri", (128, 128))
    idn_d = din("idn", (128, 128))
    e2_d = din("e2", (128, 2))
    ones_d = din("ones", (128, 512))
    vini_d = din("vini", (128, 2 * 16 * 65))
    scl_d = din("scl", (128, 1))
    shab_d = din("shab", (2, 256))

    out_d = nc.dram_tensor("out", [BC, L, DM], F32, kind="ExternalOutput").ap()

    es = ExitStack()

    def sb(name, shape, dt):
        return es.enter_context(nc.sbuf_tensor("s_" + name, list(shape), dt))

    def ps(name):
        return es.enter_context(nc.psum_tensor(name, [128, 512], F32))

    xnT = sb("xnT", (128, NFC * L), BF16)
    onT = sb("onT", (128, NP * L), BF16)
    UB = sb("UB", (128, (2 * GRP * L + GRP * 2 * 16 * 65) // 2), F32)
    UBb = UB[:, :].bitcast(BF16)

    def qrotg(j):
        return UBb[:, j * L:(j + 1) * L]

    def krotg(j):
        return UBb[:, GRP * L + j * L: GRP * L + (j + 1) * L]

    def vhatg(j):
        return UBb[:, 2 * GRP * L + j * 2080: 2 * GRP * L + (j + 1) * 2080]

    acc = UB
    sums = sb("sums", (16, L), BF16)
    rsbf = sb("rsbf", (16, L), BF16)
    s64 = sb("s64", (128, 4096), BF16)
    Ct = sb("Ct", (128, L), F32)
    St = sb("St", (128, L), F32)
    tri = sb("tri_s", (128, 128), BF16)
    idb = sb("idb", (128, 128), BF16)
    e2 = sb("e2_s", (128, 2), BF16)
    ones_r = sb("ones_r", (128, 512), BF16)
    bqp = sb("bqp", (128, 8), F32)
    bqs = sb("bqs", (128, 8), F32)
    bkp = sb("bkp", (128, 8), F32)
    bks = sb("bks", (128, 8), F32)
    bvn = sb("bvn", (128, 8), F32)
    bo_sb = sb("bo_sb", (1, DM), BF16)
    gb_sb = sb("gb_sb", (1, 8), BF16)
    eb_sb = sb("eb_sb", (8, DM), BF16)
    gw_sb = sb("gw_sb", (128, 64), BF16)
    scl = sb("scl_sb", (128, 1), F32)
    ws0 = sb("ws0", (128, 5 * DM), BF16)
    ws1 = sb("ws1", (128, 5 * DM), BF16)
    ptA0 = sb("ptA0", (128, 512), BF16)
    ptA1 = sb("ptA1", (128, 512), BF16)
    ptB0 = sb("ptB0", (128, 512), BF16)
    ptB1 = sb("ptB1", (128, 512), BF16)
    t1b = sb("t1b", (128, 512), F32)
    t2b = sb("t2b", (128, 512), F32)
    vtmp = sb("vtmp", (128, 512), BF16)
    obt0 = sb("obt0", (64, 512), BF16)
    obt1 = sb("obt1", (64, 512), BF16)
    xn_tok0 = sb("xn_tok0", (128, DM), BF16)
    xn_tok1 = sb("xn_tok1", (128, DM), BF16)
    st12 = sb("st12", (128, 12), F32)
    mvar = sb("mvar", (128, 2), F32)
    lr1 = sb("lr1", (128, 1), F32)
    lr2 = sb("lr2", (128, 1), F32)
    rstd = sb("rstd", (128, 1), F32)
    nmr = sb("nmr", (128, 1), F32)
    gex = sb("gex", (128, 8), F32)
    gmx = sb("gmx", (128, 1), F32)
    gmn = sb("gmn", (128, 1), F32)
    gsm = sb("gsm", (128, 1), F32)
    gr1 = sb("gr1", (128, 1), F32)
    gn_ = sb("gn", (128, 8), F32)
    gmk = sb("gmk", (128, 8), F32)
    gw1 = sb("gw1", (128, 8), F32)
    wcol = sb("wcol", (128, NT * 8), F32)
    wbf = sb("wbf", (128, 8), BF16)
    wT = sb("wT", (8, NT * 128), BF16)
    qmax = sb("qmax", (2, GRP), F32)
    kmax = sb("kmax", (2, GRP), F32)
    kmw = sb("kmw", (2, 1), F32)
    mh2 = sb("mh2", (2, GRP), F32)
    mh2b = sb("mh2b", (2, GRP), BF16)
    shab = sb("shab", (2, 256), BF16)
    negmh = sb("negmh", (128, 2 * GRP), F32)

    qsqbK = ptA0
    qsqbQ = ptA1

    P = [ps("P%d" % i) for i in range(8)]

    sem_names = ["st", "sv", "ss", "sld", "swp0", "swp1", "sxe0", "sxe1",
                 "sxf", "ssh0", "ssh1", "sst", "sxa0", "sxa1", "ssum", "srs"]
    sems = {n: es.enter_context(nc.semaphore(n)) for n in sem_names}
    em = Em(sems)

    AI = AF.Identity

    def const_loads(g):
        g.dma_start(out=Ct[:, :], in_=ct_d).then_inc(sems["sld"], 16)
        g.dma_start(out=St[:, :], in_=st_d).then_inc(sems["sld"], 16)
        g.dma_start(out=tri[:, :], in_=tri_d).then_inc(sems["sld"], 16)
        g.dma_start(out=idb[:, :], in_=idn_d).then_inc(sems["sld"], 16)
        g.dma_start(out=e2[:, :], in_=e2_d).then_inc(sems["sld"], 16)
        g.dma_start(out=ones_r[:, :], in_=ones_d).then_inc(sems["sld"], 16)
        for dst, src in ((bqp, bqp_d), (bqs, bqs_d), (bkp, bkp_d),
                         (bks, bks_d), (bvn, bvn_d)):
            g.dma_start(out=dst[:, :], in_=src).then_inc(sems["sld"], 16)
        g.dma_start(out=bo_sb[:, :], in_=bo_d).then_inc(sems["sld"], 16)
        g.dma_start(out=gb_sb[:, :], in_=gb_d).then_inc(sems["sld"], 16)
        g.dma_start(out=eb_sb[:, :], in_=eb_d).then_inc(sems["sld"], 16)
        g.dma_start(out=gw_sb[:, :], in_=gw_d).then_inc(sems["sld"], 16)
        for j in range(GRP):
            g.dma_start(out=vhatg(j)[:, :], in_=vini_d).then_inc(sems["sld"], 16)
        g.dma_start(out=scl[:, :], in_=scl_d).then_inc(sems["sld"], 16)
        g.dma_start(out=shab[:, :], in_=shab_d).then_inc(sems["sld"], 16)

    em.op("g", const_loads)
    em.c["sld"] += 16 * 21
    N_CONST = em.val("sld")
    for eng in ("t", "v", "s"):
        em.wait_val(eng, "sld", N_CONST)

    wslot = [ws0, ws1]
    wpsem = ["swp0", "swp1"]
    xesem = ["sxe0", "sxe1"]
    ptA = [ptA0, ptA1]
    ptB = [ptB0, ptB1]
    obt = [obt0, obt1]
    shsem = ["ssh0", "ssh1"]
    xn_tok = [xn_tok0, xn_tok1]
    xasem = ["sxa0", "sxa1"]

    rel = {}

    def issue_pair_bundle(s, dc, wait_cur=False):
        slot = dc % 2
        if wait_cur:
            em.wait("g", "st")
        else:
            em.wait_val("g", "st", rel.get(("wp", s, dc), 0))
        sem = sems[wpsem[slot]]

        def f(g, dc=dc, slot=slot, sem=sem):
            sl = wslot[slot]
            g.dma_start(out=sl[:, 0:DM], in_=wqp_d[dc]).then_inc(sem, 16)
            g.dma_start(out=sl[:, DM:2 * DM], in_=wqs_d[dc]).then_inc(sem, 16)
            g.dma_start(out=sl[:, 2 * DM:3 * DM], in_=wkp_d[dc]).then_inc(sem, 16)
            g.dma_start(out=sl[:, 3 * DM:4 * DM], in_=wks_d[dc]).then_inc(sem, 16)
            g.dma_start(out=sl[:, 4 * DM:5 * DM], in_=wvn_d[dc]).then_inc(sem, 16)

        em.op("g", f)
        em.c[wpsem[slot]] += 80

    n_stores = 0
    for s in range(BC):
        if phases < 1:
            continue
        em.barrier()
        for tt in range(NT):
            slot = tt % 2
            axv = acc[:, slot * 1024:(slot + 1) * 1024]
            em.wait_val("g", "ss", rel.get(("xa", s, tt), 0))
            em.wait_val("g", "sst", rel.get(("accst",), 0))
            em.op("g", lambda g, axv=axv, tt=tt, s=s:
                  g.dma_start(out=axv, in_=x_d[s, tt * 128:(tt + 1) * 128, :]),
                  inc=xasem[slot], n=16)
            em.wait("v", xasem[slot])
            em.op("v", lambda v, axv=axv: v.bn_stats(st12[:, 0:6], axv[:, 0:512]))
            em.op("v", lambda v, axv=axv: v.bn_stats(st12[:, 6:12], axv[:, 512:1024]),
                  inc="sv")
            em.self_sync("v")
            em.op("v", lambda v: v.bn_aggr(mvar[:, :], st12[:, :]), inc="sv")
            em.self_sync("v")
            em.op("v", lambda v: v.tensor_scalar(lr1[:, :], mvar[:, 1:2], 1e-5, None, op0=ALU.add))
            em.op("v", lambda v: v.reciprocal(lr2[:, :], lr1[:, :]), inc="sv")
            em.wait("s", "sv")
            em.op("s", lambda sc: sc.activation(rstd[:, :], lr2[:, :], AF.Sqrt), inc="ss")
            em.wait("v", "ss")
            em.op("v", lambda v: v.tensor_scalar(nmr[:, :], mvar[:, 0:1], rstd[:, :], -1.0,
                                                 op0=ALU.mult, op1=ALU.mult), inc="sv")
            em.wait("s", "sv")
            em.op("s", lambda sc, axv=axv, slot=slot:
                  sc.activation(xn_tok[slot][:, :], axv, AI,
                                bias=nmr[:, :], scale=rstd[:, :]), inc="ss")
            rel[("xa", s, tt + 2)] = em.val("ss")
            bank = P[slot]
            em.wait("t", "ss")
            em.wait_val("t", "ss", rel.get(("xnev", s, tt), 0))
            for fc in range(NFC):
                em.op("t", lambda t, slot=slot, fc=fc, bank=bank:
                      t.transpose(bank[:, :].bitcast(BF16)[:, fc * 128:(fc + 1) * 128],
                                  xn_tok[slot][:, fc * 128:(fc + 1) * 128], idb[:, :]),
                      inc="st" if fc == NFC - 1 else None)
            em.wait("s", "st")
            for fc in range(NFC):
                em.op("s", lambda sc, fc=fc, tt=tt, bank=bank:
                      sc.activation(xnT[:, fc * L + tt * 128: fc * L + (tt + 1) * 128],
                                    bank[:, :].bitcast(BF16)[:, fc * 128:(fc + 1) * 128], AI),
                      inc="ss" if fc == NFC - 1 else None)
            rel[("xnev", s, tt + 2)] = em.val("ss")
            xnev_tt = em.val("ss")

            if phases < 2:
                continue
            bank = P[2]
            em.wait_val("t", "ss", xnev_tt)
            em.wait_val("t", "ss", rel.get(("gate", s, tt), 0))
            for fc in range(NFC):
                em.op("t", lambda t, fc=fc, tt=tt, bank=bank:
                      t.matmul(bank[:, 0:8], xnT[:, fc * L + tt * 128: fc * L + (tt + 1) * 128],
                               gw_sb[:, fc * 8:(fc + 1) * 8],
                               start=(fc == 0), stop=False))
            em.op("t", lambda t, bank=bank:
                  t.matmul(bank[:, 0:8], ones_r[0:1, 0:128], gb_sb[0:1, 0:8],
                           start=False, stop=True), inc="st")
            em.wait("v", "st")
            em.op("v", lambda v, bank=bank: v.reduce_max(gmx[:, :], bank[:, 0:8], axis=AX.X))
            em.op("v", lambda v: v.tensor_scalar(gmn[:, :], gmx[:, :], -1.0, None, op0=ALU.mult),
                  inc="sv")
            em.wait("s", "sv")
            em.op("s", lambda sc, bank=bank:
                  sc.activation(gex[:, :], bank[:, 0:8], AF.Exp, bias=gmn[:, :],
                                accum_out=gsm[:, :]), inc="ss")
            rel[("gate", s, tt + 1)] = em.val("ss")
            em.wait("v", "ss")
            em.op("v", lambda v: v.reciprocal(gr1[:, :], gsm[:, :]))
            em.op("v", lambda v: v.tensor_scalar(gn_[:, :], gex[:, :], gr1[:, :], None, op0=ALU.mult))
            em.op("v", lambda v: v.tensor_scalar(gmk[:, :], gn_[:, :], 0.1, None, op0=ALU.is_ge))
            em.op("v", lambda v: v.tensor_tensor(gw1[:, :], gn_[:, :], gmk[:, :], op=ALU.mult))
            em.op("v", lambda v: v.reduce_sum(gsm[:, :], gw1[:, :], axis=AX.X))
            em.op("v", lambda v: v.reciprocal(gr1[:, :], gsm[:, :]))
            em.op("v", lambda v, tt=tt: v.tensor_scalar(wcol[:, tt * 8:(tt + 1) * 8], gw1[:, :],
                                                        gr1[:, :], None, op0=ALU.mult))
            em.op("v", lambda v: v.tensor_scalar(wbf[:, :], gw1[:, :], gr1[:, :], None,
                                                 op0=ALU.mult), inc="sv")
            em.wait("t", "sv")
            em.wait_val("t", "ss", rel.get(("wTev", s, tt), 0))
            em.op("t", lambda t: t.transpose(P[3][:, :].bitcast(BF16)[0:8, 0:128],
                                             wbf[:, :], idb[:, :]), inc="st")
            em.wait("s", "st")
            em.op("s", lambda sc, tt=tt:
                  sc.activation(wT[0:8, tt * 128:(tt + 1) * 128],
                                P[3][:, :].bitcast(BF16)[0:8, 0:128], AI), inc="ss")
            rel[("wTev", s, tt + 1)] = em.val("ss")

        if phases < 3:
            continue
        for grp in range(2):
            em.barrier()
            em.wait_val("v", "sst", rel.get(("accst",), 0))
            if grp == 0:
                issue_pair_bundle(s, 0, wait_cur=True)
                issue_pair_bundle(s, 1, wait_cur=True)
            dcs = list(range(grp * GRP, grp * GRP + GRP))
            for j, dc in enumerate(dcs):
                slot = dc % 2
                sl = wslot[slot]
                em.wait("t", wpsem[slot])

                def proj(psum, woff, win, sl=sl, inc=None):
                    for fc in range(NFC):
                        em.op("t", lambda t, psum=psum, woff=woff, fc=fc, win=win, sl=sl:
                              t.matmul(psum[:, 0:512],
                                       sl[:, woff + fc * 128: woff + (fc + 1) * 128],
                                       xnT[:, fc * L + win * 512: fc * L + (win + 1) * 512],
                                       start=(fc == 0), stop=(fc == NFC - 1)),
                              inc=inc if fc == NFC - 1 else None)

                def emit_statq(pq):
                    sqv_p, j_p, win_p = pq
                    em.wait_val("t", "ss", sqv_p)
                    em.wait_val("t", "sv", rel.get(("kstatread",), 0))
                    stqv = em.op("t", lambda t:
                                 t.matmul(P[7][0:2, 0:512], e2[:, :], qsqbQ[:, :],
                                          start=True, stop=True), inc="st")
                    rel[("stqv",)] = stqv
                    em.wait_val("v", "st", stqv)
                    if win_p == 0:
                        em.op("v", lambda v, j_p=j_p:
                              v.reduce_max(qmax[:, j_p:j_p + 1], P[7][0:2, 0:512], axis=AX.X),
                              inc="sv")
                    else:
                        em.op("v", lambda v: v.reduce_max(kmw[:, :], P[7][0:2, 0:512], axis=AX.X))
                        em.op("v", lambda v, j_p=j_p:
                              v.tensor_tensor(qmax[:, j_p:j_p + 1], qmax[:, j_p:j_p + 1],
                                              kmw[:, :], op=ALU.max), inc="sv")
                    rel[("statfree",)] = em.val("sv")

                for win in range(NWIN):
                    wsl = slice(win * 512, (win + 1) * 512)
                    em.wait_val("t", "sv", rel.get(("kfree",), 0))
                    proj(P[0], 2 * DM, win)
                    proj(P[1], 3 * DM, win, inc="st")
                    em.wait("v", "st")
                    em.op("v", lambda v, dc=dc, wsl=wsl:
                          v.scalar_tensor_tensor(t1b[:, :], P[0][:, 0:512], bkp[:, dc:dc + 1],
                                                 Ct[:, wsl], op0=ALU.add, op1=ALU.mult))
                    em.op("v", lambda v, dc=dc, wsl=wsl:
                          v.scalar_tensor_tensor(t2b[:, :], P[1][:, 0:512], bks[:, dc:dc + 1],
                                                 St[:, wsl], op0=ALU.add, op1=ALU.mult))
                    em.op("v", lambda v, j=j, wsl=wsl:
                          v.tensor_tensor(krotg(j)[:, wsl], t1b[:, :], t2b[:, :], op=ALU.add),
                          inc="sv")
                    rel[("kfree",)] = em.val("sv")
                    em.wait("s", "sv")
                    em.wait_val("s", "st", rel.get(("stkv",), 0))
                    skv = em.op("s", lambda sc, j=j, wsl=wsl:
                                sc.activation(qsqbK[:, :], krotg(j)[:, wsl], AF.Square),
                                inc="ss")
                    if rel.get(("pendq",)) is not None:
                        emit_statq(rel[("pendq",)])
                        rel[("pendq",)] = None
                    em.wait_val("t", "sv", rel.get(("vfree",), 0))
                    proj(P[6], 4 * DM, win, inc="st")
                    pvv = em.val("st")
                    em.wait_val("v", "st", pvv)
                    em.op("v", lambda v, dc=dc:
                          v.tensor_scalar(vtmp[:, :], P[6][:, 0:512], bvn[:, dc:dc + 1], None,
                                          op0=ALU.add), inc="sv")
                    vtmpv = em.val("sv")
                    rel[("vfree",)] = vtmpv
                    em.wait_val("t", "sv", rel.get(("qfree",), 0))
                    proj(P[2], 0, win)
                    proj(P[3], DM, win, inc="st")
                    em.wait("v", "st")
                    em.op("v", lambda v, dc=dc, wsl=wsl:
                          v.scalar_tensor_tensor(t1b[:, :], P[2][:, 0:512], bqp[:, dc:dc + 1],
                                                 Ct[:, wsl], op0=ALU.add, op1=ALU.mult))
                    em.op("v", lambda v, dc=dc, wsl=wsl:
                          v.scalar_tensor_tensor(t2b[:, :], P[3][:, 0:512], bqs[:, dc:dc + 1],
                                                 St[:, wsl], op0=ALU.add, op1=ALU.mult))
                    em.op("v", lambda v: v.tensor_tensor(t1b[:, :], t1b[:, :], t2b[:, :], op=ALU.add))
                    em.op("v", lambda v, j=j, wsl=wsl:
                          v.tensor_scalar(qrotg(j)[:, wsl], t1b[:, :], scl[:, :], None,
                                          op0=ALU.mult), inc="sv")
                    rel[("qfree",)] = em.val("sv")
                    em.wait("s", "sv")
                    em.wait_val("s", "st", rel.get(("stqv",), 0))
                    sqv = em.op("s", lambda sc, j=j, wsl=wsl:
                                sc.activation(qsqbQ[:, :], qrotg(j)[:, wsl], AF.Square),
                                inc="ss")
                    rel[("pendq",)] = (sqv, j, win)
                    em.wait_val("t", "ss", skv)
                    em.wait_val("t", "sv", rel.get(("statfree",), 0))
                    stkv = em.op("t", lambda t:
                                 t.matmul(P[7][0:2, 0:512], e2[:, :], qsqbK[:, :],
                                          start=True, stop=True), inc="st")
                    rel[("stkv",)] = stkv
                    em.wait_val("v", "st", stkv)
                    if win == 0:
                        em.op("v", lambda v, j=j:
                              v.reduce_max(kmax[:, j:j + 1], P[7][0:2, 0:512], axis=AX.X),
                              inc="sv")
                    else:
                        em.op("v", lambda v: v.reduce_max(kmw[:, :], P[7][0:2, 0:512], axis=AX.X))
                        em.op("v", lambda v, j=j:
                              v.tensor_tensor(kmax[:, j:j + 1], kmax[:, j:j + 1], kmw[:, :],
                                              op=ALU.max), inc="sv")
                    rel[("kstatread",)] = em.val("sv")
                    em.wait_val("t", "sv", vtmpv)
                    em.wait_val("t", "ss", rel.get(("vtfree",), 0))
                    for h2 in range(2):
                        for jj in range(4):
                            em.op("t", lambda t, h2=h2, jj=jj:
                                  t.transpose(P[4 + h2][:, :].bitcast(BF16)[:, jj * 64:
                                                                            (jj + 1) * 64],
                                              vtmp[h2 * 64:(h2 + 1) * 64, jj * 128:(jj + 1) * 128],
                                              idb[h2 * 64:(h2 + 1) * 64, h2 * 64:(h2 + 1) * 64]),
                                  inc="st" if (h2 == 1 and jj == 3) else None)
                    em.wait("s", "st")
                    for h2 in range(2):
                        for jj in range(4):
                            kcg = win * 4 + jj
                            em.op("s", lambda sc, h2=h2, jj=jj, kcg=kcg, j=j:
                                  sc.activation(vhatg(j)[:, (h2 * 16 + kcg) * 65:
                                                         (h2 * 16 + kcg) * 65 + 64],
                                                P[4 + h2][:, :].bitcast(BF16)[:, jj * 64:
                                                                              (jj + 1) * 64],
                                                AI),
                                  inc="ss" if (h2 == 1 and jj == 3) else None)
                    rel[("vtfree",)] = em.val("ss")
                rel[("vhev", s, dc)] = em.val("ss")
                if dc + 2 < NP:
                    rel[("wp", s, dc + 2)] = em.val("st")
                    issue_pair_bundle(s, dc + 2)
            if phases < 3.5:
                continue
            if rel.get(("pendq",)) is not None:
                emit_statq(rel[("pendq",)])
                rel[("pendq",)] = None
            em.op("v", lambda v: v.tensor_tensor(mh2[:, :], qmax[:, :], kmax[:, :],
                                                 op=ALU.mult), inc="sv")
            em.wait("s", "sv")
            em.op("s", lambda sc: sc.activation(mh2[:, :], mh2[:, :], AF.Sqrt), inc="ss")
            em.wait("v", "ss")
            em.op("v", lambda v: v.tensor_scalar(mh2b[:, :], mh2[:, :], -1.0, None,
                                                 op0=ALU.mult), inc="sv")
            em.wait("t", "sv")
            em.op("t", lambda t: t.matmul(P[7][:, 0:GRP], shab[:, 0:128], mh2b[:, 0:GRP],
                                          start=True, stop=True))
            em.op("t", lambda t: t.matmul(P[7][:, GRP:2 * GRP], shab[:, 128:256],
                                          mh2b[:, 0:GRP], start=True, stop=True), inc="st")
            em.wait("v", "st")
            em.op("v", lambda v: v.tensor_copy(negmh[:, :], P[7][:, 0:2 * GRP]), inc="sv")
            nmv = em.val("sv")

            if phases < 4:
                continue
            SA = [P[2], P[3]]
            SB = [P[4], P[5]]
            em.wait("t", "ss")
            for j, dc in enumerate(dcs):
                em.wait_val("t", "ss", rel[("vhev", s, dc)])
                em.wait_val("t", "sv", nmv)
                for win in range(NWIN):
                    nkc = 4 * win + 4
                    for stage in range(nkc + 1):
                        if stage < nkc:
                            kc = stage
                            kb = kc % 2
                            off = max(0, (kc - 4 * win) * 128)
                            diag = kc >= 4 * win
                            qs0 = win * 512 + off
                            qs1 = (win + 1) * 512
                            em.wait_val("t", "ss", rel.get(("sexp", kb), 0))
                            for (Sp, base) in ((SA[kb], 0), (SB[kb], 64)):
                                lastg = (base == 64) and not diag
                                em.op("t", lambda t, Sp=Sp, base=base, kc=kc, off=off, qs0=qs0,
                                            qs1=qs1, diag=diag, j=j:
                                      t.matmul(Sp[:, off:512],
                                               krotg(j)[base:base + 64, kc * 128:(kc + 1) * 128],
                                               qrotg(j)[base:base + 64, qs0:qs1],
                                               start=True, stop=not diag),
                                      inc="st" if lastg else None)
                                if diag:
                                    em.op("t", lambda t, Sp=Sp, off=off:
                                          t.matmul(Sp[:, off:off + 128], idb[:, :], tri[:, :],
                                                   start=False, stop=True),
                                          inc="st" if base == 64 else None)
                            em.wait("s", "st")
                            em.op("s", lambda sc, kb=kb, off=off, j=j:
                                  sc.activation(ptA[kb][:, off:512], SA[kb][:, off:512], AF.Exp,
                                                bias=negmh[:, j:j + 1]))
                            em.op("s", lambda sc, kb=kb, off=off, j=j:
                                  sc.activation(ptB[kb][:, off:512], SB[kb][:, off:512], AF.Exp,
                                                bias=negmh[:, GRP + j:GRP + j + 1]), inc="ss")
                            rel[("sexp", kb)] = em.val("ss")
                            rel[("expv", j, kc)] = em.val("ss")
                        if stage >= 1:
                            kc = stage - 1
                            kb = kc % 2
                            off = max(0, (kc - 4 * win) * 128)
                            em.wait_val("t", "ss", rel[("expv", j, kc)])
                            em.op("t", lambda t, kb=kb, kc=kc, off=off, j=j,
                                        last=(kc == nkc - 1):
                                  t.matmul(P[6][0:65, off:512],
                                           vhatg(j)[:, kc * 65: kc * 65 + 65],
                                           ptA[kb][:, off:512], start=(kc == 0), stop=last))
                            em.op("t", lambda t, kb=kb, kc=kc, off=off, j=j,
                                        last=(kc == nkc - 1):
                                  t.matmul(P[7][0:65, off:512],
                                           vhatg(j)[:, (16 + kc) * 65: (16 + kc) * 65 + 65],
                                           ptB[kb][:, off:512], start=(kc == 0), stop=last),
                                  inc="st")
                    em.wait("s", "st")
                    em.wait_val("s", shsem[win % 2], 16 * rel.get(("shcnt", win % 2), 0))
                    em.wait_val("s", "ssum", 32 * rel.get(("sumcnt",), 0))
                    em.op("s", lambda sc, win=win:
                          sc.activation(s64[64:65, win * 512:(win + 1) * 512],
                                        P[6][64:65, 0:512], AI))
                    em.op("s", lambda sc, win=win:
                          sc.activation(s64[64:65, 2048 + win * 512: 2048 + (win + 1) * 512],
                                        P[7][64:65, 0:512], AI))
                    em.op("s", lambda sc, dc=dc, win=win:
                          sc.activation(onT[0:64, dc * L + win * 512: dc * L + (win + 1) * 512],
                                        P[6][0:64, 0:512], AI))
                    em.op("s", lambda sc, win=win:
                          sc.activation(obt[win % 2][:, :], P[7][0:64, 0:512], AI), inc="ss")
                    em.wait("g", "ss")
                    em.op("g", lambda g, dc=dc, win=win:
                          g.dma_start(out=onT[64:128, dc * L + win * 512: dc * L + (win + 1) * 512],
                                      in_=obt[win % 2][:, :]), inc=shsem[win % 2], n=16)
                    rel[("shcnt", win % 2)] = rel.get(("shcnt", win % 2), 0) + 1
                em.wait_val("g", "sv", rel.get(("sumfree",), 0))
                em.op("g", lambda g, dc=dc:
                      g.dma_start(out=sums[2 * dc:2 * dc + 1, 0:L], in_=s64[64:65, 0:L]),
                      inc="ssum", n=16)
                em.op("g", lambda g, dc=dc:
                      g.dma_start(out=sums[2 * dc + 1:2 * dc + 2, 0:L], in_=s64[64:65, L:2 * L]),
                      inc="ssum", n=16)
                rel[("sumcnt",)] = rel.get(("sumcnt",), 0) + 1

        if phases < 5:
            continue
        em.barrier()
        em.wait("v", "ssh0")
        em.wait("v", "ssh1")
        em.wait("v", "ssum")
        def _recip(v, nc=nc):
            with nc.allow_low_precision(reason="recip in f32 internally, one rounding"):
                return v.reciprocal(rsbf[:, :], sums[:, :])

        em.op("v", _recip, inc="sv")
        rel[("sumfree",)] = em.val("sv")
        rsbase = [0, 32]
        for dc in range(NP):
            em.wait("g", "sv")
            em.wait_val("g", "st", rel.get(("rsfree", dc % 2), 0))
            rb = rsbase[dc % 2]
            em.op("g", lambda g, dc=dc, rb=rb: g.dma_start(out=s64[rb:rb + 1, 0:L],
                                                           in_=rsbf[2 * dc:2 * dc + 1, 0:L]),
                  inc="srs", n=16)
            rsw = em.op("g", lambda g, dc=dc, rb=rb:
                        g.dma_start(out=s64[rb:rb + 1, L:2 * L],
                                    in_=rsbf[2 * dc + 1:2 * dc + 2, 0:L]),
                        inc="srs", n=16)
            em.wait_val("t", "srs", rsw)
            for win in range(NWIN):
                par = (dc * NWIN + win) % 2
                bank = P[2 + par]
                em.wait_val("t", "sv", rel.get(("obc", par), 0))
                em.op("t", lambda t, win=win, bank=bank, rb=rb:
                      t.matmul(bank[0:64, 0:512], ones_r[rb:rb + 1, 0:64],
                               s64[rb:rb + 1, win * 512:(win + 1) * 512],
                               start=True, stop=True))
                em.op("t", lambda t, win=win, bank=bank, rb=rb:
                      t.matmul(bank[64:128, 0:512], ones_r[rb:rb + 1, 0:64],
                               s64[rb:rb + 1, L + win * 512: L + (win + 1) * 512],
                               start=True, stop=True), inc="st")
                if win == NWIN - 1:
                    rel[("rsfree", dc % 2)] = em.val("st")
                em.wait("v", "st")
                em.op("v", lambda v, dc=dc, win=win, bank=bank:
                      v.tensor_tensor(onT[:, dc * L + win * 512: dc * L + (win + 1) * 512],
                                      onT[:, dc * L + win * 512: dc * L + (win + 1) * 512],
                                      bank[:, 0:512], op=ALU.mult), inc="sv")
                rel[("obc", par)] = em.val("sv")

        if phases < 6:
            continue
        em.barrier()
        for icol in range(2):
            em.wait_val("g", "sst", rel.get(("accst",), 0))
            em.wait("g", "sv")
            for tt in range(NT):
                em.op("g", lambda g, s=s, tt=tt, icol=icol:
                      g.dma_start(out=acc[:, tt * 512:(tt + 1) * 512],
                                  in_=x_d[s, tt * 128:(tt + 1) * 128,
                                          icol * 512:(icol + 1) * 512]),
                      inc="sxf", n=16)
            xe_wait = {}
            st0 = em.val("st")
            for e in range(9):
                slot = e % 2
                em.wait_val("g", "st", st0 if e < 2 else st0 + (e - 1) * 16)
                xe_wait[e] = em.op("g", lambda g, e=e, icol=icol, slot=slot:
                                   g.dma_start(out=wslot[slot][:, 0:NFC * 512],
                                               in_=wex_d[e, icol]),
                                   inc=xesem[slot], n=16)
            for e in range(9):
                slot = e % 2
                em.wait_val("t", xesem[slot], xe_wait[e])
                for tt in range(NT):
                    par = tt % 2
                    bank = P[par]
                    em.wait_val("t", "sv", rel.get(("fps", par), 0))
                    src = xnT if e < 8 else onT
                    for fc in range(NFC):
                        em.op("t", lambda t, bank=bank, fc=fc, tt=tt, slot=slot, src=src, e=e:
                              t.matmul(bank[:, 0:512],
                                       src[:, fc * L + tt * 128: fc * L + (tt + 1) * 128],
                                       wslot[slot][:, fc * 512:(fc + 1) * 512],
                                       start=(fc == 0), stop=(e < 8 and fc == NFC - 1)),
                              inc="st" if (e < 8 and fc == NFC - 1) else None)
                    if e == 8:
                        em.op("t", lambda t, bank=bank, icol=icol:
                              t.matmul(bank[:, 0:512], ones_r[0:1, 0:128],
                                       bo_sb[0:1, icol * 512:(icol + 1) * 512],
                                       start=False, stop=False))
                        em.op("t", lambda t, bank=bank, tt=tt, icol=icol:
                              t.matmul(bank[:, 0:512], wT[0:8, tt * 128:(tt + 1) * 128],
                                       eb_sb[0:8, icol * 512:(icol + 1) * 512],
                                       start=False, stop=True), inc="st")
                    em.wait("v", "st")
                    if e == 0:
                        em.wait("v", "sxf")
                        em.wait_val("v", "sst", rel.get(("accst",), 0))
                    asl = acc[:, tt * 512:(tt + 1) * 512]
                    if e < 8:
                        em.op("v", lambda v, bank=bank, tt=tt, e=e, asl=asl:
                              v.scalar_tensor_tensor(asl, bank[:, 0:512],
                                                     wcol[:, tt * 8 + e: tt * 8 + e + 1],
                                                     asl, op0=ALU.mult, op1=ALU.add), inc="sv")
                    else:
                        em.op("v", lambda v, bank=bank, asl=asl:
                              v.tensor_tensor(asl, asl, bank[:, 0:512], op=ALU.add), inc="sv")
                    rel[("fps", par)] = em.val("sv")
                    if e == 8:
                        em.wait("y", "sv")
                        em.op("y", lambda y, s=s, tt=tt, icol=icol:
                              y.dma_start(out=out_d[s, tt * 128:(tt + 1) * 128,
                                                    icol * 512:(icol + 1) * 512],
                                          in_=acc[:, tt * 512:(tt + 1) * 512]),
                              inc="sst", n=16)
                        n_stores += 1
            rel[("accst",)] = em.val("sst")

    em.wait_val("y", "sst", 16 * n_stores)

    with nc.Block() as block:
        @block.tensor
        def _(t):
            for f in em.L["t"]:
                f(t)

        @block.vector
        def _(v):
            for f in em.L["v"]:
                f(v)

        @block.scalar
        def _(sc):
            for f in em.L["s"]:
                f(sc)

        @block.gpsimd
        def _(g):
            for f in em.L["g"]:
                f(g)

        @block.sync
        def _(y):
            for f in em.L["y"]:
                f(y)

    es.close()
    return nc


def _host_inputs(L0_attn_Wq, L0_attn_bq, L0_attn_Wk, L0_attn_bk, L0_attn_Wv,
                 L0_attn_bv, L0_attn_Wo, L0_attn_bo, L0_attn_scale,
                 L0_ffn_gate_w, L0_ffn_gate_b, L0_ffn_expert_w, L0_ffn_expert_b):
    f32 = np.float32
    Wq = np.asarray(L0_attn_Wq, f32)
    Wk = np.asarray(L0_attn_Wk, f32)
    Wv = np.asarray(L0_attn_Wv, f32)
    Wo = np.asarray(L0_attn_Wo, f32)
    bq = np.asarray(L0_attn_bq, f32)
    bk = np.asarray(L0_attn_bk, f32)
    bv = np.asarray(L0_attn_bv, f32)
    bo = np.asarray(L0_attn_bo, f32)
    scale = np.asarray(L0_attn_scale, f32)
    gw = np.asarray(L0_ffn_gate_w, f32)
    gb = np.asarray(L0_ffn_gate_b, f32)
    ew = np.asarray(L0_ffn_expert_w, f32)
    eb = np.asarray(L0_ffn_expert_b, f32)

    pi = np.zeros(DM, np.int64)
    sig = np.zeros(DM, np.int64)
    for n in range(NH):
        b0 = n * HD
        pi[b0:b0 + 32] = b0 + 2 * np.arange(32)
        pi[b0 + 32:b0 + 64] = b0 + 2 * np.arange(32) + 1
        sig[b0 + 2 * np.arange(32)] = b0 + 2 * np.arange(32) + 1
        sig[b0 + 2 * np.arange(32) + 1] = b0 + 2 * np.arange(32)

    def pair_img(WT):
        return np.ascontiguousarray(
            WT.reshape(8, 128, 8, 128).transpose(2, 1, 0, 3).reshape(8, 128, 1024))

    ins = {}
    ins["wqp"] = pair_img(Wq.T[:, pi])
    ins["wqs"] = pair_img(Wq.T[:, sig[pi]])
    ins["wkp"] = pair_img(Wk.T[:, pi])
    ins["wks"] = pair_img(Wk.T[:, sig[pi]])
    ins["wvn"] = pair_img(Wv.T)
    wex = np.zeros((9, 2, 128, NFC * 512), f32)
    for e in range(8):
        wex[e] = ew[e].reshape(8, 128, 2, 512).transpose(2, 1, 0, 3).reshape(2, 128, NFC * 512)
    wex[8] = Wo.T.reshape(8, 128, 2, 512).transpose(2, 1, 0, 3).reshape(2, 128, NFC * 512)
    ins["wex"] = np.ascontiguousarray(wex)
    ins["gw"] = np.ascontiguousarray(gw.reshape(8, 128, 8).transpose(1, 0, 2).reshape(128, 64))
    ins["bqp"] = np.ascontiguousarray(bq[pi].reshape(8, 128).T)
    ins["bqs"] = np.ascontiguousarray(bq[sig[pi]].reshape(8, 128).T)
    ins["bkp"] = np.ascontiguousarray(bk[pi].reshape(8, 128).T)
    ins["bks"] = np.ascontiguousarray(bk[sig[pi]].reshape(8, 128).T)
    ins["bvn"] = np.ascontiguousarray(bv.reshape(8, 128).T)
    ins["bo"] = bo.reshape(1, DM)
    ins["gb"] = gb.reshape(1, 8)
    ins["eb"] = eb
    ins["scl"] = np.repeat(scale.reshape(1, 1), 128, axis=0)

    freqs = 10000.0 ** (-np.arange(0, HD, 2, dtype=np.float64) / HD)
    ang = np.arange(L, dtype=np.float64)[:, None] * freqs[None, :]
    cosT = np.cos(ang).astype(f32).T
    sinT = np.sin(ang).astype(f32).T
    C64 = np.concatenate([cosT, cosT], 0)
    S64 = np.concatenate([-sinT, sinT], 0)
    ins["ct"] = np.ascontiguousarray(np.concatenate([C64, C64], 0))
    ins["stb"] = np.ascontiguousarray(np.concatenate([S64, S64], 0))

    k_i = np.arange(128)
    ins["tri"] = np.where(k_i[None, :] < k_i[:, None], f32(-65536.0), f32(0.0))
    ins["idn"] = np.eye(128, dtype=f32)
    e2c = np.zeros((128, 2), f32)
    e2c[0:64, 0] = 1.0
    e2c[64:128, 1] = 1.0
    ins["e2"] = e2c
    ins["ones"] = np.ones((128, 512), f32)
    vini = np.zeros((128, 2 * 16 * 65), f32)
    vini[:, 64::65] = 1.0
    ins["vini"] = vini
    shab = np.zeros((2, 256), f32)
    shab[0, 0:128] = 1.0
    shab[1, 128:256] = 1.0
    ins["shab"] = shab
    return ins


def kernel(L0_attn_x, L0_attn_Wq, L0_attn_bq, L0_attn_Wk, L0_attn_bk,
           L0_attn_Wv, L0_attn_bv, L0_attn_Wo, L0_attn_bo, L0_attn_scale,
           L0_ffn_gate_w, L0_ffn_gate_b, L0_ffn_expert_w, L0_ffn_expert_b):
    x = np.ascontiguousarray(np.asarray(L0_attn_x, np.float32))
    shared = _host_inputs(L0_attn_Wq, L0_attn_bq, L0_attn_Wk, L0_attn_bk,
                          L0_attn_Wv, L0_attn_bv, L0_attn_Wo, L0_attn_bo,
                          L0_attn_scale, L0_ffn_gate_w, L0_ffn_gate_b,
                          L0_ffn_expert_w, L0_ffn_expert_b)
    nc = build_program()
    in_maps = []
    for c in range(8):
        m = dict(shared)
        m["x"] = x[2 * c:2 * c + 2]
        in_maps.append(m)
    res = run_bass_kernel_spmd(nc, in_maps, core_ids=list(range(8)))
    out = np.concatenate([res.results[c]["out"] for c in range(8)], axis=0)
    return out


# version 09
# speedup vs baseline: 1.0475x; 1.0475x over previous
import numpy as np
from contextlib import ExitStack

import concourse.bass as bass
import concourse.mybir as mybir
from concourse.bass_utils import run_bass_kernel_spmd

F32 = mybir.dt.float32
BF16 = mybir.dt.bfloat16
AF = mybir.ActivationFunctionType
ALU = mybir.AluOpType
AX = mybir.AxisListType

BC = 2
L = 2048
DM = 1024
NH = 16
HD = 64
NP = 8
NWIN = 4
NT = 16
NFC = 8
GRP = 4


class Em:

    ENGS = ("t", "v", "s", "g", "y")
    OWN = {"t": "st", "v": "sv", "s": "ss"}

    def __init__(self, sems):
        self.L = {k: [] for k in self.ENGS}
        self.sems = sems
        self.c = {k: 0 for k in sems}
        self.lastwait = {}

    def val(self, name):
        return self.c[name]

    def bump(self, name, n=1):
        self.c[name] += n
        return self.c[name]

    def wait_val(self, eng, name, v):
        if v <= 0:
            return
        if self.OWN.get(eng) == name:
            return
        key = (eng, name)
        if self.lastwait.get(key, 0) >= v:
            return
        self.lastwait[key] = v
        sem = self.sems[name]
        self.L[eng].append(lambda e, sem=sem, v=v: e.wait_ge(sem, v))

    def wait(self, eng, name):
        self.wait_val(eng, name, self.val(name))

    def op(self, eng, fn, inc=None, n=1):
        if eng == "v":
            own = self.OWN[eng]
            pv = self.val(own)
            sem0 = self.sems[own]
            if pv > 0:
                self.L[eng].append(lambda e, sem0=sem0, pv=pv: e.wait_ge(sem0, pv))
            inc = own
            n = 1
        if inc is None:
            self.L[eng].append(fn)
            return None
        v = self.bump(inc, n)
        sem = self.sems[inc]
        self.L[eng].append(lambda e, fn=fn, sem=sem, n=n: fn(e).then_inc(sem, n))
        return v

    def self_sync(self, eng):
        name = self.OWN[eng]
        v = self.val(name)
        sem = self.sems[name]
        self.L[eng].append(lambda e, sem=sem, v=v: e.wait_ge(sem, v))

    def barrier(self, engs=("t", "v", "s")):
        vals = {k: self.val(v) for k, v in self.OWN.items()}
        for eng in engs:
            for src, name in self.OWN.items():
                if src != eng:
                    self.wait_val(eng, name, vals[src])


def build_program(phases=99.0):
    nc = bass.Bass("TRN2", target_bir_lowering=False, detect_race_conditions=False)

    def din(name, shape):
        return nc.dram_tensor(name, list(shape), F32, kind="ExternalInput").ap()

    x_d = din("x", (BC, L, DM))
    wqp_d = din("wqp", (NP, 128, DM))
    wqs_d = din("wqs", (NP, 128, DM))
    wkp_d = din("wkp", (NP, 128, DM))
    wks_d = din("wks", (NP, 128, DM))
    wvn_d = din("wvn", (NP, 128, DM))
    wex_d = din("wex", (9, 2, 128, NFC * 512))
    gw_d = din("gw", (128, 64))
    bqp_d = din("bqp", (128, 8))
    bqs_d = din("bqs", (128, 8))
    bkp_d = din("bkp", (128, 8))
    bks_d = din("bks", (128, 8))
    bvn_d = din("bvn", (128, 8))
    bo_d = din("bo", (1, DM))
    gb_d = din("gb", (1, 8))
    eb_d = din("eb", (8, DM))
    ct_d = din("ct", (128, L))
    st_d = din("stb", (128, L))
    tri_d = din("tri", (128, 128))
    idn_d = din("idn", (128, 128))
    e2_d = din("e2", (128, 2))
    ones_d = din("ones", (128, 512))
    vini_d = din("vini", (128, 2 * 16 * 65))
    scl_d = din("scl", (128, 1))
    shab_d = din("shab", (2, 256))

    out_d = nc.dram_tensor("out", [BC, L, DM], F32, kind="ExternalOutput").ap()

    es = ExitStack()

    def sb(name, shape, dt):
        return es.enter_context(nc.sbuf_tensor("s_" + name, list(shape), dt))

    def ps(name):
        return es.enter_context(nc.psum_tensor(name, [128, 512], F32))

    xnT = sb("xnT", (128, NFC * L), BF16)
    onT = sb("onT", (128, NP * L), BF16)
    UB = sb("UB", (128, (2 * GRP * L + GRP * 2 * 16 * 65) // 2), F32)
    UBb = UB[:, :].bitcast(BF16)

    def qrotg(j):
        return UBb[:, j * L:(j + 1) * L]

    def krotg(j):
        return UBb[:, GRP * L + j * L: GRP * L + (j + 1) * L]

    def vhatg(j):
        return UBb[:, 2 * GRP * L + j * 2080: 2 * GRP * L + (j + 1) * 2080]

    acc = UB
    sums = sb("sums", (16, L), BF16)
    rsbf = sb("rsbf", (16, L), BF16)
    s64 = sb("s64", (128, 4096), BF16)
    Ct = sb("Ct", (128, L), F32)
    St = sb("St", (128, L), F32)
    tri = sb("tri_s", (128, 128), BF16)
    idb = sb("idb", (128, 128), BF16)
    e2 = sb("e2_s", (128, 2), BF16)
    ones_r = sb("ones_r", (128, 512), BF16)
    bqp = sb("bqp", (128, 8), F32)
    bqs = sb("bqs", (128, 8), F32)
    bkp = sb("bkp", (128, 8), F32)
    bks = sb("bks", (128, 8), F32)
    bvn = sb("bvn", (128, 8), F32)
    bo_sb = sb("bo_sb", (1, DM), BF16)
    gb_sb = sb("gb_sb", (1, 8), BF16)
    eb_sb = sb("eb_sb", (8, DM), BF16)
    gw_sb = sb("gw_sb", (128, 64), BF16)
    scl = sb("scl_sb", (128, 1), F32)
    ws0 = sb("ws0", (128, 5 * DM), BF16)
    ws1 = sb("ws1", (128, 5 * DM), BF16)
    ptA0 = sb("ptA0", (128, 512), BF16)
    ptA1 = sb("ptA1", (128, 512), BF16)
    ptB0 = sb("ptB0", (128, 512), BF16)
    ptB1 = sb("ptB1", (128, 512), BF16)
    t1b = sb("t1b", (128, 512), F32)
    t2b = sb("t2b", (128, 512), F32)
    vtmp = sb("vtmp", (128, 512), BF16)
    obt0 = sb("obt0", (64, 512), BF16)
    obt1 = sb("obt1", (64, 512), BF16)
    xn_tok0 = sb("xn_tok0", (128, DM), BF16)
    xn_tok1 = sb("xn_tok1", (128, DM), BF16)
    st12 = sb("st12", (128, 12), F32)
    mvar = sb("mvar", (128, 2), F32)
    lr1 = sb("lr1", (128, 1), F32)
    lr2 = sb("lr2", (128, 1), F32)
    rstd = sb("rstd", (128, 1), F32)
    nmr = sb("nmr", (128, 1), F32)
    gex = sb("gex", (128, 8), F32)
    gmx = sb("gmx", (128, 1), F32)
    gmn = sb("gmn", (128, 1), F32)
    gsm = sb("gsm", (128, 1), F32)
    gr1 = sb("gr1", (128, 1), F32)
    gn_ = sb("gn", (128, 8), F32)
    gmk = sb("gmk", (128, 8), F32)
    gw1 = sb("gw1", (128, 8), F32)
    wcol = sb("wcol", (128, NT * 8), F32)
    wbf = sb("wbf", (128, 8), BF16)
    wbf1 = sb("wbf1", (128, 8), BF16)
    wbfp = [wbf, wbf1]
    wT = sb("wT", (8, NT * 128), BF16)
    qmax = sb("qmax", (2, GRP), F32)
    kmax = sb("kmax", (2, GRP), F32)
    kmw = sb("kmw", (2, 1), F32)
    mh2 = sb("mh2", (2, GRP), F32)
    mh2b = sb("mh2b", (2, GRP), BF16)
    shab = sb("shab", (2, 256), BF16)
    negmh = sb("negmh", (128, 2 * GRP), F32)

    qsqbK = ptA0
    qsqbQ = ptA1

    P = [ps("P%d" % i) for i in range(8)]

    sem_names = ["st", "sv", "ss", "sld", "swp0", "swp1", "sxe0", "sxe1",
                 "sxf", "ssh0", "ssh1", "sst", "sxa0", "sxa1", "ssum", "srs"]
    sems = {n: es.enter_context(nc.semaphore(n)) for n in sem_names}
    em = Em(sems)

    AI = AF.Identity

    def const_loads(g):
        g.dma_start(out=Ct[:, :], in_=ct_d).then_inc(sems["sld"], 16)
        g.dma_start(out=St[:, :], in_=st_d).then_inc(sems["sld"], 16)
        g.dma_start(out=tri[:, :], in_=tri_d).then_inc(sems["sld"], 16)
        g.dma_start(out=idb[:, :], in_=idn_d).then_inc(sems["sld"], 16)
        g.dma_start(out=e2[:, :], in_=e2_d).then_inc(sems["sld"], 16)
        g.dma_start(out=ones_r[:, :], in_=ones_d).then_inc(sems["sld"], 16)
        for dst, src in ((bqp, bqp_d), (bqs, bqs_d), (bkp, bkp_d),
                         (bks, bks_d), (bvn, bvn_d)):
            g.dma_start(out=dst[:, :], in_=src).then_inc(sems["sld"], 16)
        g.dma_start(out=bo_sb[:, :], in_=bo_d).then_inc(sems["sld"], 16)
        g.dma_start(out=gb_sb[:, :], in_=gb_d).then_inc(sems["sld"], 16)
        g.dma_start(out=eb_sb[:, :], in_=eb_d).then_inc(sems["sld"], 16)
        g.dma_start(out=gw_sb[:, :], in_=gw_d).then_inc(sems["sld"], 16)
        for j in range(GRP):
            g.dma_start(out=vhatg(j)[:, :], in_=vini_d).then_inc(sems["sld"], 16)
        g.dma_start(out=scl[:, :], in_=scl_d).then_inc(sems["sld"], 16)
        g.dma_start(out=shab[:, :], in_=shab_d).then_inc(sems["sld"], 16)

    em.op("g", const_loads)
    em.c["sld"] += 16 * 21
    N_CONST = em.val("sld")
    for eng in ("t", "v", "s"):
        em.wait_val(eng, "sld", N_CONST)

    wslot = [ws0, ws1]
    wpsem = ["swp0", "swp1"]
    xesem = ["sxe0", "sxe1"]
    ptA = [ptA0, ptA1]
    ptB = [ptB0, ptB1]
    obt = [obt0, obt1]
    shsem = ["ssh0", "ssh1"]
    xn_tok = [xn_tok0, xn_tok1]
    xasem = ["sxa0", "sxa1"]

    rel = {}

    def issue_pair_bundle(s, dc, wait_cur=False):
        slot = dc % 2
        if wait_cur:
            em.wait("g", "st")
        else:
            em.wait_val("g", "st", rel.get(("wp", s, dc), 0))
        sem = sems[wpsem[slot]]

        def f(g, dc=dc, slot=slot, sem=sem):
            sl = wslot[slot]
            g.dma_start(out=sl[:, 0:DM], in_=wqp_d[dc]).then_inc(sem, 16)
            g.dma_start(out=sl[:, DM:2 * DM], in_=wqs_d[dc]).then_inc(sem, 16)
            g.dma_start(out=sl[:, 2 * DM:3 * DM], in_=wkp_d[dc]).then_inc(sem, 16)
            g.dma_start(out=sl[:, 3 * DM:4 * DM], in_=wks_d[dc]).then_inc(sem, 16)
            g.dma_start(out=sl[:, 4 * DM:5 * DM], in_=wvn_d[dc]).then_inc(sem, 16)

        em.op("g", f)
        em.c[wpsem[slot]] += 80

    n_stores = 0
    for s in range(BC):
        if phases < 1:
            continue
        em.barrier()
        for tt in range(NT):
            slot = tt % 2
            axv = acc[:, slot * 1024:(slot + 1) * 1024]
            em.wait_val("g", "ss", rel.get(("xa", s, tt), 0))
            em.wait_val("g", "sst", rel.get(("accst",), 0))
            em.op("g", lambda g, axv=axv, tt=tt, s=s:
                  g.dma_start(out=axv, in_=x_d[s, tt * 128:(tt + 1) * 128, :]),
                  inc=xasem[slot], n=16)
            em.wait("v", xasem[slot])
            em.op("v", lambda v, axv=axv: v.bn_stats(st12[:, 0:6], axv[:, 0:512]))
            em.op("v", lambda v, axv=axv: v.bn_stats(st12[:, 6:12], axv[:, 512:1024]),
                  inc="sv")
            em.self_sync("v")
            em.op("v", lambda v: v.bn_aggr(mvar[:, :], st12[:, :]), inc="sv")
            em.self_sync("v")
            em.op("v", lambda v: v.tensor_scalar(lr1[:, :], mvar[:, 1:2], 1e-5, None, op0=ALU.add))
            em.op("v", lambda v: v.reciprocal(lr2[:, :], lr1[:, :]), inc="sv")
            em.wait("s", "sv")
            em.op("s", lambda sc: sc.activation(rstd[:, :], lr2[:, :], AF.Sqrt), inc="ss")
            em.wait("v", "ss")
            em.op("v", lambda v: v.tensor_scalar(nmr[:, :], mvar[:, 0:1], rstd[:, :], -1.0,
                                                 op0=ALU.mult, op1=ALU.mult), inc="sv")
            em.wait("s", "sv")
            em.op("s", lambda sc, axv=axv, slot=slot:
                  sc.activation(xn_tok[slot][:, :], axv, AI,
                                bias=nmr[:, :], scale=rstd[:, :]), inc="ss")
            rel[("xa", s, tt + 2)] = em.val("ss")
            bank = P[slot]
            em.wait("t", "ss")
            em.wait_val("t", "ss", rel.get(("xnev", s, tt), 0))
            for fc in range(NFC):
                em.op("t", lambda t, slot=slot, fc=fc, bank=bank:
                      t.transpose(bank[:, :].bitcast(BF16)[:, fc * 128:(fc + 1) * 128],
                                  xn_tok[slot][:, fc * 128:(fc + 1) * 128], idb[:, :]),
                      inc="st" if fc == NFC - 1 else None)
            em.wait("s", "st")
            for fc in range(NFC):
                em.op("s", lambda sc, fc=fc, tt=tt, bank=bank:
                      sc.activation(xnT[:, fc * L + tt * 128: fc * L + (tt + 1) * 128],
                                    bank[:, :].bitcast(BF16)[:, fc * 128:(fc + 1) * 128], AI),
                      inc="ss" if fc == NFC - 1 else None)
            rel[("xnev", s, tt + 2)] = em.val("ss")

        if phases < 2:
            continue
        em.barrier()
        for tt in range(NT):
            par = tt % 2
            bank = P[2 + 2 * par]
            gbank = P[3 + 2 * par]
            wb = wbfp[par]
            em.wait_val("t", "ss", rel.get(("gate", s, tt), 0))
            for fc in range(NFC):
                em.op("t", lambda t, fc=fc, tt=tt, bank=bank:
                      t.matmul(bank[:, 0:8], xnT[:, fc * L + tt * 128: fc * L + (tt + 1) * 128],
                               gw_sb[:, fc * 8:(fc + 1) * 8],
                               start=(fc == 0), stop=False))
            em.op("t", lambda t, bank=bank:
                  t.matmul(bank[:, 0:8], ones_r[0:1, 0:128], gb_sb[0:1, 0:8],
                           start=False, stop=True), inc="st")
            em.wait("v", "st")
            em.op("v", lambda v, bank=bank: v.reduce_max(gmx[:, :], bank[:, 0:8], axis=AX.X))
            em.op("v", lambda v: v.tensor_scalar(gmn[:, :], gmx[:, :], -1.0, None, op0=ALU.mult),
                  inc="sv")
            em.wait("s", "sv")
            em.op("s", lambda sc, bank=bank:
                  sc.activation(gex[:, :], bank[:, 0:8], AF.Exp, bias=gmn[:, :],
                                accum_out=gsm[:, :]), inc="ss")
            rel[("gate", s, tt + 2)] = em.val("ss")
            em.wait("v", "ss")
            em.op("v", lambda v: v.reciprocal(gr1[:, :], gsm[:, :]))
            em.op("v", lambda v: v.tensor_scalar(gn_[:, :], gex[:, :], gr1[:, :], None, op0=ALU.mult))
            em.op("v", lambda v: v.tensor_scalar(gmk[:, :], gn_[:, :], 0.1, None, op0=ALU.is_ge))
            em.op("v", lambda v: v.tensor_tensor(gw1[:, :], gn_[:, :], gmk[:, :], op=ALU.mult))
            em.op("v", lambda v: v.reduce_sum(gsm[:, :], gw1[:, :], axis=AX.X))
            em.op("v", lambda v: v.reciprocal(gr1[:, :], gsm[:, :]))
            em.op("v", lambda v, tt=tt: v.tensor_scalar(wcol[:, tt * 8:(tt + 1) * 8], gw1[:, :],
                                                        gr1[:, :], None, op0=ALU.mult))
            em.op("v", lambda v, wb=wb: v.tensor_scalar(wb[:, :], gw1[:, :], gr1[:, :], None,
                                                        op0=ALU.mult), inc="sv")
            em.wait("t", "sv")
            em.wait_val("t", "ss", rel.get(("wTev", s, tt), 0))
            em.op("t", lambda t, gbank=gbank, wb=wb:
                  t.transpose(gbank[:, :].bitcast(BF16)[0:8, 0:128],
                              wb[:, :], idb[:, :]), inc="st")
            em.wait("s", "st")
            em.op("s", lambda sc, tt=tt, gbank=gbank:
                  sc.activation(wT[0:8, tt * 128:(tt + 1) * 128],
                                gbank[:, :].bitcast(BF16)[0:8, 0:128], AI), inc="ss")
            rel[("wTev", s, tt + 2)] = em.val("ss")

        if phases < 3:
            continue
        for grp in range(2):
            em.barrier()
            em.wait_val("v", "sst", rel.get(("accst",), 0))
            if grp == 0:
                issue_pair_bundle(s, 0, wait_cur=True)
                issue_pair_bundle(s, 1, wait_cur=True)
            dcs = list(range(grp * GRP, grp * GRP + GRP))
            for j, dc in enumerate(dcs):
                slot = dc % 2
                sl = wslot[slot]
                em.wait("t", wpsem[slot])

                def proj(psum, woff, win, sl=sl, inc=None):
                    for fc in range(NFC):
                        em.op("t", lambda t, psum=psum, woff=woff, fc=fc, win=win, sl=sl:
                              t.matmul(psum[:, 0:512],
                                       sl[:, woff + fc * 128: woff + (fc + 1) * 128],
                                       xnT[:, fc * L + win * 512: fc * L + (win + 1) * 512],
                                       start=(fc == 0), stop=(fc == NFC - 1)),
                              inc=inc if fc == NFC - 1 else None)

                def emit_statq(pq):
                    sqv_p, j_p, win_p = pq
                    em.wait_val("t", "ss", sqv_p)
                    em.wait_val("t", "sv", rel.get(("kstatread",), 0))
                    stqv = em.op("t", lambda t:
                                 t.matmul(P[7][0:2, 0:512], e2[:, :], qsqbQ[:, :],
                                          start=True, stop=True), inc="st")
                    rel[("stqv",)] = stqv
                    em.wait_val("v", "st", stqv)
                    if win_p == 0:
                        em.op("v", lambda v, j_p=j_p:
                              v.reduce_max(qmax[:, j_p:j_p + 1], P[7][0:2, 0:512], axis=AX.X),
                              inc="sv")
                    else:
                        em.op("v", lambda v: v.reduce_max(kmw[:, :], P[7][0:2, 0:512], axis=AX.X))
                        em.op("v", lambda v, j_p=j_p:
                              v.tensor_tensor(qmax[:, j_p:j_p + 1], qmax[:, j_p:j_p + 1],
                                              kmw[:, :], op=ALU.max), inc="sv")
                    rel[("statfree",)] = em.val("sv")

                for win in range(NWIN):
                    wsl = slice(win * 512, (win + 1) * 512)
                    em.wait_val("t", "sv", rel.get(("kfree",), 0))
                    proj(P[0], 2 * DM, win)
                    proj(P[1], 3 * DM, win, inc="st")
                    em.wait("v", "st")
                    em.op("v", lambda v, dc=dc, wsl=wsl:
                          v.scalar_tensor_tensor(t1b[:, :], P[0][:, 0:512], bkp[:, dc:dc + 1],
                                                 Ct[:, wsl], op0=ALU.add, op1=ALU.mult))
                    em.op("v", lambda v, dc=dc, wsl=wsl:
                          v.scalar_tensor_tensor(t2b[:, :], P[1][:, 0:512], bks[:, dc:dc + 1],
                                                 St[:, wsl], op0=ALU.add, op1=ALU.mult))
                    em.op("v", lambda v, j=j, wsl=wsl:
                          v.tensor_tensor(krotg(j)[:, wsl], t1b[:, :], t2b[:, :], op=ALU.add),
                          inc="sv")
                    rel[("kfree",)] = em.val("sv")
                    em.wait("s", "sv")
                    em.wait_val("s", "st", rel.get(("stkv",), 0))
                    skv = em.op("s", lambda sc, j=j, wsl=wsl:
                                sc.activation(qsqbK[:, :], krotg(j)[:, wsl], AF.Square),
                                inc="ss")
                    if rel.get(("pendq",)) is not None:
                        emit_statq(rel[("pendq",)])
                        rel[("pendq",)] = None
                    em.wait_val("t", "sv", rel.get(("vfree",), 0))
                    proj(P[6], 4 * DM, win, inc="st")
                    pvv = em.val("st")
                    em.wait_val("v", "st", pvv)
                    em.op("v", lambda v, dc=dc:
                          v.tensor_scalar(vtmp[:, :], P[6][:, 0:512], bvn[:, dc:dc + 1], None,
                                          op0=ALU.add), inc="sv")
                    vtmpv = em.val("sv")
                    rel[("vfree",)] = vtmpv
                    em.wait_val("t", "sv", rel.get(("qfree",), 0))
                    proj(P[2], 0, win)
                    proj(P[3], DM, win, inc="st")
                    em.wait("v", "st")
                    em.op("v", lambda v, dc=dc, wsl=wsl:
                          v.scalar_tensor_tensor(t1b[:, :], P[2][:, 0:512], bqp[:, dc:dc + 1],
                                                 Ct[:, wsl], op0=ALU.add, op1=ALU.mult))
                    em.op("v", lambda v, dc=dc, wsl=wsl:
                          v.scalar_tensor_tensor(t2b[:, :], P[3][:, 0:512], bqs[:, dc:dc + 1],
                                                 St[:, wsl], op0=ALU.add, op1=ALU.mult))
                    em.op("v", lambda v: v.tensor_tensor(t1b[:, :], t1b[:, :], t2b[:, :], op=ALU.add))
                    em.op("v", lambda v, j=j, wsl=wsl:
                          v.tensor_scalar(qrotg(j)[:, wsl], t1b[:, :], scl[:, :], None,
                                          op0=ALU.mult), inc="sv")
                    rel[("qfree",)] = em.val("sv")
                    em.wait("s", "sv")
                    em.wait_val("s", "st", rel.get(("stqv",), 0))
                    sqv = em.op("s", lambda sc, j=j, wsl=wsl:
                                sc.activation(qsqbQ[:, :], qrotg(j)[:, wsl], AF.Square),
                                inc="ss")
                    rel[("pendq",)] = (sqv, j, win)
                    em.wait_val("t", "ss", skv)
                    em.wait_val("t", "sv", rel.get(("statfree",), 0))
                    stkv = em.op("t", lambda t:
                                 t.matmul(P[7][0:2, 0:512], e2[:, :], qsqbK[:, :],
                                          start=True, stop=True), inc="st")
                    rel[("stkv",)] = stkv
                    em.wait_val("v", "st", stkv)
                    if win == 0:
                        em.op("v", lambda v, j=j:
                              v.reduce_max(kmax[:, j:j + 1], P[7][0:2, 0:512], axis=AX.X),
                              inc="sv")
                    else:
                        em.op("v", lambda v: v.reduce_max(kmw[:, :], P[7][0:2, 0:512], axis=AX.X))
                        em.op("v", lambda v, j=j:
                              v.tensor_tensor(kmax[:, j:j + 1], kmax[:, j:j + 1], kmw[:, :],
                                              op=ALU.max), inc="sv")
                    rel[("kstatread",)] = em.val("sv")
                    em.wait_val("t", "sv", vtmpv)
                    em.wait_val("t", "ss", rel.get(("vtfree",), 0))
                    for h2 in range(2):
                        for jj in range(4):
                            em.op("t", lambda t, h2=h2, jj=jj:
                                  t.transpose(P[4 + h2][:, :].bitcast(BF16)[:, jj * 64:
                                                                            (jj + 1) * 64],
                                              vtmp[h2 * 64:(h2 + 1) * 64, jj * 128:(jj + 1) * 128],
                                              idb[h2 * 64:(h2 + 1) * 64, h2 * 64:(h2 + 1) * 64]),
                                  inc="st" if (h2 == 1 and jj == 3) else None)
                    em.wait("s", "st")
                    for h2 in range(2):
                        for jj in range(4):
                            kcg = win * 4 + jj
                            em.op("s", lambda sc, h2=h2, jj=jj, kcg=kcg, j=j:
                                  sc.activation(vhatg(j)[:, (h2 * 16 + kcg) * 65:
                                                         (h2 * 16 + kcg) * 65 + 64],
                                                P[4 + h2][:, :].bitcast(BF16)[:, jj * 64:
                                                                              (jj + 1) * 64],
                                                AI),
                                  inc="ss" if (h2 == 1 and jj == 3) else None)
                    rel[("vtfree",)] = em.val("ss")
                rel[("vhev", s, dc)] = em.val("ss")
                if dc + 2 < NP:
                    rel[("wp", s, dc + 2)] = em.val("st")
                    issue_pair_bundle(s, dc + 2)
            if phases < 3.5:
                continue
            if rel.get(("pendq",)) is not None:
                emit_statq(rel[("pendq",)])
                rel[("pendq",)] = None
            em.op("v", lambda v: v.tensor_tensor(mh2[:, :], qmax[:, :], kmax[:, :],
                                                 op=ALU.mult), inc="sv")
            em.wait("s", "sv")
            em.op("s", lambda sc: sc.activation(mh2[:, :], mh2[:, :], AF.Sqrt), inc="ss")
            em.wait("v", "ss")
            em.op("v", lambda v: v.tensor_scalar(mh2b[:, :], mh2[:, :], -1.0, None,
                                                 op0=ALU.mult), inc="sv")
            em.wait("t", "sv")
            em.op("t", lambda t: t.matmul(P[7][:, 0:GRP], shab[:, 0:128], mh2b[:, 0:GRP],
                                          start=True, stop=True))
            em.op("t", lambda t: t.matmul(P[7][:, GRP:2 * GRP], shab[:, 128:256],
                                          mh2b[:, 0:GRP], start=True, stop=True), inc="st")
            em.wait("v", "st")
            em.op("v", lambda v: v.tensor_copy(negmh[:, :], P[7][:, 0:2 * GRP]), inc="sv")
            nmv = em.val("sv")

            if phases < 4:
                continue
            SA = [P[2], P[3]]
            SB = [P[4], P[5]]
            em.wait("t", "ss")
            for j, dc in enumerate(dcs):
                em.wait_val("t", "ss", rel[("vhev", s, dc)])
                em.wait_val("t", "sv", nmv)
                for win in range(NWIN):
                    nkc = 4 * win + 4
                    for stage in range(nkc + 1):
                        if stage < nkc:
                            kc = stage
                            kb = kc % 2
                            off = max(0, (kc - 4 * win) * 128)
                            diag = kc >= 4 * win
                            qs0 = win * 512 + off
                            qs1 = (win + 1) * 512
                            em.wait_val("t", "ss", rel.get(("sexp", kb), 0))
                            for (Sp, base) in ((SA[kb], 0), (SB[kb], 64)):
                                lastg = (base == 64) and not diag
                                em.op("t", lambda t, Sp=Sp, base=base, kc=kc, off=off, qs0=qs0,
                                            qs1=qs1, diag=diag, j=j:
                                      t.matmul(Sp[:, off:512],
                                               krotg(j)[base:base + 64, kc * 128:(kc + 1) * 128],
                                               qrotg(j)[base:base + 64, qs0:qs1],
                                               start=True, stop=not diag),
                                      inc="st" if lastg else None)
                                if diag:
                                    em.op("t", lambda t, Sp=Sp, off=off:
                                          t.matmul(Sp[:, off:off + 128], idb[:, :], tri[:, :],
                                                   start=False, stop=True),
                                          inc="st" if base == 64 else None)
                            em.wait("s", "st")
                            em.op("s", lambda sc, kb=kb, off=off, j=j:
                                  sc.activation(ptA[kb][:, off:512], SA[kb][:, off:512], AF.Exp,
                                                bias=negmh[:, j:j + 1]))
                            em.op("s", lambda sc, kb=kb, off=off, j=j:
                                  sc.activation(ptB[kb][:, off:512], SB[kb][:, off:512], AF.Exp,
                                                bias=negmh[:, GRP + j:GRP + j + 1]), inc="ss")
                            rel[("sexp", kb)] = em.val("ss")
                            rel[("expv", j, kc)] = em.val("ss")
                        if stage >= 1:
                            kc = stage - 1
                            kb = kc % 2
                            off = max(0, (kc - 4 * win) * 128)
                            em.wait_val("t", "ss", rel[("expv", j, kc)])
                            em.op("t", lambda t, kb=kb, kc=kc, off=off, j=j,
                                        last=(kc == nkc - 1):
                                  t.matmul(P[6][0:65, off:512],
                                           vhatg(j)[:, kc * 65: kc * 65 + 65],
                                           ptA[kb][:, off:512], start=(kc == 0), stop=last))
                            em.op("t", lambda t, kb=kb, kc=kc, off=off, j=j,
                                        last=(kc == nkc - 1):
                                  t.matmul(P[7][0:65, off:512],
                                           vhatg(j)[:, (16 + kc) * 65: (16 + kc) * 65 + 65],
                                           ptB[kb][:, off:512], start=(kc == 0), stop=last),
                                  inc="st")
                    em.wait("s", "st")
                    em.wait_val("s", shsem[win % 2], 16 * rel.get(("shcnt", win % 2), 0))
                    em.wait_val("s", "ssum", 32 * rel.get(("sumcnt",), 0))
                    em.op("s", lambda sc, win=win:
                          sc.activation(s64[64:65, win * 512:(win + 1) * 512],
                                        P[6][64:65, 0:512], AI))
                    em.op("s", lambda sc, win=win:
                          sc.activation(s64[64:65, 2048 + win * 512: 2048 + (win + 1) * 512],
                                        P[7][64:65, 0:512], AI))
                    em.op("s", lambda sc, dc=dc, win=win:
                          sc.activation(onT[0:64, dc * L + win * 512: dc * L + (win + 1) * 512],
                                        P[6][0:64, 0:512], AI))
                    em.op("s", lambda sc, win=win:
                          sc.activation(obt[win % 2][:, :], P[7][0:64, 0:512], AI), inc="ss")
                    em.wait("g", "ss")
                    em.op("g", lambda g, dc=dc, win=win:
                          g.dma_start(out=onT[64:128, dc * L + win * 512: dc * L + (win + 1) * 512],
                                      in_=obt[win % 2][:, :]), inc=shsem[win % 2], n=16)
                    rel[("shcnt", win % 2)] = rel.get(("shcnt", win % 2), 0) + 1
                em.wait_val("g", "sv", rel.get(("sumfree",), 0))
                em.op("g", lambda g, dc=dc:
                      g.dma_start(out=sums[2 * dc:2 * dc + 1, 0:L], in_=s64[64:65, 0:L]),
                      inc="ssum", n=16)
                em.op("g", lambda g, dc=dc:
                      g.dma_start(out=sums[2 * dc + 1:2 * dc + 2, 0:L], in_=s64[64:65, L:2 * L]),
                      inc="ssum", n=16)
                rel[("sumcnt",)] = rel.get(("sumcnt",), 0) + 1

        if phases < 5:
            continue
        em.barrier()
        em.wait("v", "ssh0")
        em.wait("v", "ssh1")
        em.wait("v", "ssum")
        def _recip(v, nc=nc):
            with nc.allow_low_precision(reason="recip in f32 internally, one rounding"):
                return v.reciprocal(rsbf[:, :], sums[:, :])

        em.op("v", _recip, inc="sv")
        rel[("sumfree",)] = em.val("sv")
        rsbase = [0, 32]
        for dc in range(NP):
            em.wait("g", "sv")
            em.wait_val("g", "st", rel.get(("rsfree", dc % 2), 0))
            rb = rsbase[dc % 2]
            em.op("g", lambda g, dc=dc, rb=rb: g.dma_start(out=s64[rb:rb + 1, 0:L],
                                                           in_=rsbf[2 * dc:2 * dc + 1, 0:L]),
                  inc="srs", n=16)
            rsw = em.op("g", lambda g, dc=dc, rb=rb:
                        g.dma_start(out=s64[rb:rb + 1, L:2 * L],
                                    in_=rsbf[2 * dc + 1:2 * dc + 2, 0:L]),
                        inc="srs", n=16)
            em.wait_val("t", "srs", rsw)
            for win in range(NWIN):
                par = (dc * NWIN + win) % 2
                bank = P[2 + par]
                em.wait_val("t", "sv", rel.get(("obc", par), 0))
                em.op("t", lambda t, win=win, bank=bank, rb=rb:
                      t.matmul(bank[0:64, 0:512], ones_r[rb:rb + 1, 0:64],
                               s64[rb:rb + 1, win * 512:(win + 1) * 512],
                               start=True, stop=True))
                em.op("t", lambda t, win=win, bank=bank, rb=rb:
                      t.matmul(bank[64:128, 0:512], ones_r[rb:rb + 1, 0:64],
                               s64[rb:rb + 1, L + win * 512: L + (win + 1) * 512],
                               start=True, stop=True), inc="st")
                if win == NWIN - 1:
                    rel[("rsfree", dc % 2)] = em.val("st")
                em.wait("v", "st")
                em.op("v", lambda v, dc=dc, win=win, bank=bank:
                      v.tensor_tensor(onT[:, dc * L + win * 512: dc * L + (win + 1) * 512],
                                      onT[:, dc * L + win * 512: dc * L + (win + 1) * 512],
                                      bank[:, 0:512], op=ALU.mult), inc="sv")
                rel[("obc", par)] = em.val("sv")

        if phases < 6:
            continue
        em.barrier()
        for icol in range(2):
            em.wait_val("g", "sst", rel.get(("accst",), 0))
            em.wait("g", "sv")
            for tt in range(NT):
                em.op("g", lambda g, s=s, tt=tt, icol=icol:
                      g.dma_start(out=acc[:, tt * 512:(tt + 1) * 512],
                                  in_=x_d[s, tt * 128:(tt + 1) * 128,
                                          icol * 512:(icol + 1) * 512]),
                      inc="sxf", n=16)
            xe_wait = {}
            st0 = em.val("st")
            for e in range(9):
                slot = e % 2
                em.wait_val("g", "st", st0 if e < 2 else st0 + (e - 1) * 16)
                xe_wait[e] = em.op("g", lambda g, e=e, icol=icol, slot=slot:
                                   g.dma_start(out=wslot[slot][:, 0:NFC * 512],
                                               in_=wex_d[e, icol]),
                                   inc=xesem[slot], n=16)
            for e in range(9):
                slot = e % 2
                em.wait_val("t", xesem[slot], xe_wait[e])
                for tt in range(NT):
                    par = tt % 2
                    bank = P[par]
                    em.wait_val("t", "sv", rel.get(("fps", par), 0))
                    src = xnT if e < 8 else onT
                    for fc in range(NFC):
                        em.op("t", lambda t, bank=bank, fc=fc, tt=tt, slot=slot, src=src, e=e:
                              t.matmul(bank[:, 0:512],
                                       src[:, fc * L + tt * 128: fc * L + (tt + 1) * 128],
                                       wslot[slot][:, fc * 512:(fc + 1) * 512],
                                       start=(fc == 0), stop=(e < 8 and fc == NFC - 1)),
                              inc="st" if (e < 8 and fc == NFC - 1) else None)
                    if e == 8:
                        em.op("t", lambda t, bank=bank, icol=icol:
                              t.matmul(bank[:, 0:512], ones_r[0:1, 0:128],
                                       bo_sb[0:1, icol * 512:(icol + 1) * 512],
                                       start=False, stop=False))
                        em.op("t", lambda t, bank=bank, tt=tt, icol=icol:
                              t.matmul(bank[:, 0:512], wT[0:8, tt * 128:(tt + 1) * 128],
                                       eb_sb[0:8, icol * 512:(icol + 1) * 512],
                                       start=False, stop=True), inc="st")
                    em.wait("v", "st")
                    if e == 0:
                        em.wait("v", "sxf")
                        em.wait_val("v", "sst", rel.get(("accst",), 0))
                    asl = acc[:, tt * 512:(tt + 1) * 512]
                    if e < 8:
                        em.op("v", lambda v, bank=bank, tt=tt, e=e, asl=asl:
                              v.scalar_tensor_tensor(asl, bank[:, 0:512],
                                                     wcol[:, tt * 8 + e: tt * 8 + e + 1],
                                                     asl, op0=ALU.mult, op1=ALU.add), inc="sv")
                    else:
                        em.op("v", lambda v, bank=bank, asl=asl:
                              v.tensor_tensor(asl, asl, bank[:, 0:512], op=ALU.add), inc="sv")
                    rel[("fps", par)] = em.val("sv")
                    if e == 8:
                        em.wait("y", "sv")
                        em.op("y", lambda y, s=s, tt=tt, icol=icol:
                              y.dma_start(out=out_d[s, tt * 128:(tt + 1) * 128,
                                                    icol * 512:(icol + 1) * 512],
                                          in_=acc[:, tt * 512:(tt + 1) * 512]),
                              inc="sst", n=16)
                        n_stores += 1
            rel[("accst",)] = em.val("sst")

    em.wait_val("y", "sst", 16 * n_stores)

    with nc.Block() as block:
        @block.tensor
        def _(t):
            for f in em.L["t"]:
                f(t)

        @block.vector
        def _(v):
            for f in em.L["v"]:
                f(v)

        @block.scalar
        def _(sc):
            for f in em.L["s"]:
                f(sc)

        @block.gpsimd
        def _(g):
            for f in em.L["g"]:
                f(g)

        @block.sync
        def _(y):
            for f in em.L["y"]:
                f(y)

    es.close()
    return nc


def _host_inputs(L0_attn_Wq, L0_attn_bq, L0_attn_Wk, L0_attn_bk, L0_attn_Wv,
                 L0_attn_bv, L0_attn_Wo, L0_attn_bo, L0_attn_scale,
                 L0_ffn_gate_w, L0_ffn_gate_b, L0_ffn_expert_w, L0_ffn_expert_b):
    f32 = np.float32
    Wq = np.asarray(L0_attn_Wq, f32)
    Wk = np.asarray(L0_attn_Wk, f32)
    Wv = np.asarray(L0_attn_Wv, f32)
    Wo = np.asarray(L0_attn_Wo, f32)
    bq = np.asarray(L0_attn_bq, f32)
    bk = np.asarray(L0_attn_bk, f32)
    bv = np.asarray(L0_attn_bv, f32)
    bo = np.asarray(L0_attn_bo, f32)
    scale = np.asarray(L0_attn_scale, f32)
    gw = np.asarray(L0_ffn_gate_w, f32)
    gb = np.asarray(L0_ffn_gate_b, f32)
    ew = np.asarray(L0_ffn_expert_w, f32)
    eb = np.asarray(L0_ffn_expert_b, f32)

    pi = np.zeros(DM, np.int64)
    sig = np.zeros(DM, np.int64)
    for n in range(NH):
        b0 = n * HD
        pi[b0:b0 + 32] = b0 + 2 * np.arange(32)
        pi[b0 + 32:b0 + 64] = b0 + 2 * np.arange(32) + 1
        sig[b0 + 2 * np.arange(32)] = b0 + 2 * np.arange(32) + 1
        sig[b0 + 2 * np.arange(32) + 1] = b0 + 2 * np.arange(32)

    def pair_img(WT):
        return np.ascontiguousarray(
            WT.reshape(8, 128, 8, 128).transpose(2, 1, 0, 3).reshape(8, 128, 1024))

    ins = {}
    ins["wqp"] = pair_img(Wq.T[:, pi])
    ins["wqs"] = pair_img(Wq.T[:, sig[pi]])
    ins["wkp"] = pair_img(Wk.T[:, pi])
    ins["wks"] = pair_img(Wk.T[:, sig[pi]])
    ins["wvn"] = pair_img(Wv.T)
    wex = np.zeros((9, 2, 128, NFC * 512), f32)
    for e in range(8):
        wex[e] = ew[e].reshape(8, 128, 2, 512).transpose(2, 1, 0, 3).reshape(2, 128, NFC * 512)
    wex[8] = Wo.T.reshape(8, 128, 2, 512).transpose(2, 1, 0, 3).reshape(2, 128, NFC * 512)
    ins["wex"] = np.ascontiguousarray(wex)
    ins["gw"] = np.ascontiguousarray(gw.reshape(8, 128, 8).transpose(1, 0, 2).reshape(128, 64))
    ins["bqp"] = np.ascontiguousarray(bq[pi].reshape(8, 128).T)
    ins["bqs"] = np.ascontiguousarray(bq[sig[pi]].reshape(8, 128).T)
    ins["bkp"] = np.ascontiguousarray(bk[pi].reshape(8, 128).T)
    ins["bks"] = np.ascontiguousarray(bk[sig[pi]].reshape(8, 128).T)
    ins["bvn"] = np.ascontiguousarray(bv.reshape(8, 128).T)
    ins["bo"] = bo.reshape(1, DM)
    ins["gb"] = gb.reshape(1, 8)
    ins["eb"] = eb
    ins["scl"] = np.repeat(scale.reshape(1, 1), 128, axis=0)

    freqs = 10000.0 ** (-np.arange(0, HD, 2, dtype=np.float64) / HD)
    ang = np.arange(L, dtype=np.float64)[:, None] * freqs[None, :]
    cosT = np.cos(ang).astype(f32).T
    sinT = np.sin(ang).astype(f32).T
    C64 = np.concatenate([cosT, cosT], 0)
    S64 = np.concatenate([-sinT, sinT], 0)
    ins["ct"] = np.ascontiguousarray(np.concatenate([C64, C64], 0))
    ins["stb"] = np.ascontiguousarray(np.concatenate([S64, S64], 0))

    k_i = np.arange(128)
    ins["tri"] = np.where(k_i[None, :] < k_i[:, None], f32(-65536.0), f32(0.0))
    ins["idn"] = np.eye(128, dtype=f32)
    e2c = np.zeros((128, 2), f32)
    e2c[0:64, 0] = 1.0
    e2c[64:128, 1] = 1.0
    ins["e2"] = e2c
    ins["ones"] = np.ones((128, 512), f32)
    vini = np.zeros((128, 2 * 16 * 65), f32)
    vini[:, 64::65] = 1.0
    ins["vini"] = vini
    shab = np.zeros((2, 256), f32)
    shab[0, 0:128] = 1.0
    shab[1, 128:256] = 1.0
    ins["shab"] = shab
    return ins


def kernel(L0_attn_x, L0_attn_Wq, L0_attn_bq, L0_attn_Wk, L0_attn_bk,
           L0_attn_Wv, L0_attn_bv, L0_attn_Wo, L0_attn_bo, L0_attn_scale,
           L0_ffn_gate_w, L0_ffn_gate_b, L0_ffn_expert_w, L0_ffn_expert_b):
    x = np.ascontiguousarray(np.asarray(L0_attn_x, np.float32))
    shared = _host_inputs(L0_attn_Wq, L0_attn_bq, L0_attn_Wk, L0_attn_bk,
                          L0_attn_Wv, L0_attn_bv, L0_attn_Wo, L0_attn_bo,
                          L0_attn_scale, L0_ffn_gate_w, L0_ffn_gate_b,
                          L0_ffn_expert_w, L0_ffn_expert_b)
    nc = build_program()
    in_maps = []
    for c in range(8):
        m = dict(shared)
        m["x"] = x[2 * c:2 * c + 2]
        in_maps.append(m)
    res = run_bass_kernel_spmd(nc, in_maps, core_ids=list(range(8)))
    out = np.concatenate([res.results[c]["out"] for c in range(8)], axis=0)
    return out


# version 10 (final)
# speedup vs baseline: 1.0592x; 1.0112x over previous
import numpy as np
from contextlib import ExitStack

import concourse.bass as bass
import concourse.mybir as mybir
from concourse.bass_utils import run_bass_kernel_spmd

F32 = mybir.dt.float32
BF16 = mybir.dt.bfloat16
AF = mybir.ActivationFunctionType
ALU = mybir.AluOpType
AX = mybir.AxisListType

BC = 2
L = 2048
DM = 1024
NH = 16
HD = 64
NP = 8
NWIN = 4
NT = 16
NFC = 8
GRP = 4


class Em:

    ENGS = ("t", "v", "s", "g", "y")
    OWN = {"t": "st", "v": "sv", "s": "ss"}

    def __init__(self, sems):
        self.L = {k: [] for k in self.ENGS}
        self.sems = sems
        self.c = {k: 0 for k in sems}
        self.lastwait = {}

    def val(self, name):
        return self.c[name]

    def bump(self, name, n=1):
        self.c[name] += n
        return self.c[name]

    def wait_val(self, eng, name, v):
        if v <= 0:
            return
        if self.OWN.get(eng) == name:
            return
        key = (eng, name)
        if self.lastwait.get(key, 0) >= v:
            return
        self.lastwait[key] = v
        sem = self.sems[name]
        self.L[eng].append(lambda e, sem=sem, v=v: e.wait_ge(sem, v))

    def wait(self, eng, name):
        self.wait_val(eng, name, self.val(name))

    def op(self, eng, fn, inc=None, n=1):
        if eng == "v":
            own = self.OWN[eng]
            pv = self.val(own)
            sem0 = self.sems[own]
            if pv > 0:
                self.L[eng].append(lambda e, sem0=sem0, pv=pv: e.wait_ge(sem0, pv))
            inc = own
            n = 1
        if inc is None:
            self.L[eng].append(fn)
            return None
        v = self.bump(inc, n)
        sem = self.sems[inc]
        self.L[eng].append(lambda e, fn=fn, sem=sem, n=n: fn(e).then_inc(sem, n))
        return v

    def self_sync(self, eng):
        name = self.OWN[eng]
        v = self.val(name)
        sem = self.sems[name]
        self.L[eng].append(lambda e, sem=sem, v=v: e.wait_ge(sem, v))

    def barrier(self, engs=("t", "v", "s")):
        vals = {k: self.val(v) for k, v in self.OWN.items()}
        for eng in engs:
            for src, name in self.OWN.items():
                if src != eng:
                    self.wait_val(eng, name, vals[src])


def build_program(phases=99.0):
    nc = bass.Bass("TRN2", target_bir_lowering=False, detect_race_conditions=False)

    def din(name, shape):
        return nc.dram_tensor(name, list(shape), F32, kind="ExternalInput").ap()

    x_d = din("x", (BC, L, DM))
    wqp_d = din("wqp", (NP, 128, DM))
    wqs_d = din("wqs", (NP, 128, DM))
    wkp_d = din("wkp", (NP, 128, DM))
    wks_d = din("wks", (NP, 128, DM))
    wvn_d = din("wvn", (NP, 128, DM))
    wex_d = din("wex", (9, 2, 128, NFC * 512))
    gw_d = din("gw", (128, 64))
    bqp_d = din("bqp", (128, 8))
    bqs_d = din("bqs", (128, 8))
    bkp_d = din("bkp", (128, 8))
    bks_d = din("bks", (128, 8))
    bvn_d = din("bvn", (128, 8))
    bo_d = din("bo", (1, DM))
    gb_d = din("gb", (1, 8))
    eb_d = din("eb", (8, DM))
    ct_d = din("ct", (128, L))
    st_d = din("stb", (128, L))
    tri_d = din("tri", (128, 128))
    idn_d = din("idn", (128, 128))
    e2_d = din("e2", (128, 2))
    ones_d = din("ones", (128, 512))
    vini_d = din("vini", (128, 2 * 16 * 65))
    scl_d = din("scl", (128, 1))

    out_d = nc.dram_tensor("out", [BC, L, DM], F32, kind="ExternalOutput").ap()

    es = ExitStack()

    def sb(name, shape, dt):
        return es.enter_context(nc.sbuf_tensor("s_" + name, list(shape), dt))

    def ps(name):
        return es.enter_context(nc.psum_tensor(name, [128, 512], F32))

    xnT = sb("xnT", (128, NFC * L), BF16)
    onT = sb("onT", (128, NP * L), BF16)
    UB = sb("UB", (128, (2 * GRP * L + GRP * 2 * 16 * 65) // 2), F32)
    UBb = UB[:, :].bitcast(BF16)

    def qrotg(j):
        return UBb[:, j * L:(j + 1) * L]

    def krotg(j):
        return UBb[:, GRP * L + j * L: GRP * L + (j + 1) * L]

    def vhatg(j):
        return UBb[:, 2 * GRP * L + j * 2080: 2 * GRP * L + (j + 1) * 2080]

    acc = UB
    sums = sb("sums", (16, L), BF16)
    rsbf = sb("rsbf", (16, L), BF16)
    s64 = sb("s64", (128, 4096), BF16)
    Ct = sb("Ct", (128, L), F32)
    St = sb("St", (128, L), F32)
    tri = sb("tri_s", (128, 128), BF16)
    idb = sb("idb", (128, 128), BF16)
    e2 = sb("e2_s", (128, 2), BF16)
    ones_r = sb("ones_r", (128, 512), BF16)
    bqp = sb("bqp", (128, 8), F32)
    bqs = sb("bqs", (128, 8), F32)
    bkp = sb("bkp", (128, 8), F32)
    bks = sb("bks", (128, 8), F32)
    bvn = sb("bvn", (128, 8), F32)
    bo_sb = sb("bo_sb", (1, DM), BF16)
    gb_sb = sb("gb_sb", (1, 8), BF16)
    eb_sb = sb("eb_sb", (8, DM), BF16)
    gw_sb = sb("gw_sb", (128, 64), BF16)
    scl = sb("scl_sb", (128, 1), F32)
    ws0 = sb("ws0", (128, 5 * DM), BF16)
    ws1 = sb("ws1", (128, 5 * DM), BF16)
    pt0 = sb("pt0", (128, 1024), BF16)
    pt1 = sb("pt1", (128, 1024), BF16)
    t1b = sb("t1b", (128, 512), F32)
    t2b = sb("t2b", (128, 512), F32)
    vtmp = sb("vtmp", (128, 512), BF16)
    obt0 = sb("obt0", (64, 512), BF16)
    obt1 = sb("obt1", (64, 512), BF16)
    xn_tok0 = sb("xn_tok0", (128, DM), BF16)
    xn_tok1 = sb("xn_tok1", (128, DM), BF16)
    st12 = sb("st12", (128, 12), F32)
    mvar = sb("mvar", (128, 2), F32)
    lr1 = sb("lr1", (128, 1), F32)
    lr2 = sb("lr2", (128, 1), F32)
    rstd = sb("rstd", (128, 1), F32)
    nmr = sb("nmr", (128, 1), F32)
    gex = sb("gex", (128, 8), F32)
    gmx = sb("gmx", (128, 1), F32)
    gmn = sb("gmn", (128, 1), F32)
    gsm = sb("gsm", (128, 1), F32)
    gr1 = sb("gr1", (128, 1), F32)
    gn_ = sb("gn", (128, 8), F32)
    gmk = sb("gmk", (128, 8), F32)
    gw1 = sb("gw1", (128, 8), F32)
    wcol = sb("wcol", (128, NT * 8), F32)
    wbf = sb("wbf", (128, 8), BF16)
    wbf1 = sb("wbf1", (128, 8), BF16)
    wbfp = [wbf, wbf1]
    wT = sb("wT", (8, NT * 128), BF16)
    qmax = sb("qmax", (1, GRP), F32)
    kmax = sb("kmax", (1, GRP), F32)
    kmw = sb("kmw", (1, 1), F32)
    mh2 = sb("mh2", (1, GRP), F32)
    mh2b = sb("mh2b", (1, GRP), BF16)
    negmh = sb("negmh", (128, GRP), F32)

    qsqbK = pt0[:, 0:512]
    qsqbQ = pt0[:, 512:1024]

    S2 = [es.enter_context(nc.psum_tensor("S2_%d" % i, [128, 1024], F32))
          for i in range(2)]
    P = [ps("P0"), ps("P1"),
         S2[0][:, 0:512], S2[0][:, 512:1024],
         S2[1][:, 0:512], S2[1][:, 512:1024],
         ps("P6"), ps("P7")]

    sem_names = ["st", "sv", "ss", "sld", "swp0", "swp1", "sxe0", "sxe1",
                 "sxf", "ssh0", "ssh1", "sst", "sxa0", "sxa1", "ssum", "srs"]
    sems = {n: es.enter_context(nc.semaphore(n)) for n in sem_names}
    em = Em(sems)

    AI = AF.Identity

    def const_loads(g):
        g.dma_start(out=Ct[:, :], in_=ct_d).then_inc(sems["sld"], 16)
        g.dma_start(out=St[:, :], in_=st_d).then_inc(sems["sld"], 16)
        g.dma_start(out=tri[:, :], in_=tri_d).then_inc(sems["sld"], 16)
        g.dma_start(out=idb[:, :], in_=idn_d).then_inc(sems["sld"], 16)
        g.dma_start(out=e2[:, :], in_=e2_d).then_inc(sems["sld"], 16)
        g.dma_start(out=ones_r[:, :], in_=ones_d).then_inc(sems["sld"], 16)
        for dst, src in ((bqp, bqp_d), (bqs, bqs_d), (bkp, bkp_d),
                         (bks, bks_d), (bvn, bvn_d)):
            g.dma_start(out=dst[:, :], in_=src).then_inc(sems["sld"], 16)
        g.dma_start(out=bo_sb[:, :], in_=bo_d).then_inc(sems["sld"], 16)
        g.dma_start(out=gb_sb[:, :], in_=gb_d).then_inc(sems["sld"], 16)
        g.dma_start(out=eb_sb[:, :], in_=eb_d).then_inc(sems["sld"], 16)
        g.dma_start(out=gw_sb[:, :], in_=gw_d).then_inc(sems["sld"], 16)
        for j in range(GRP):
            g.dma_start(out=vhatg(j)[:, :], in_=vini_d).then_inc(sems["sld"], 16)
        g.dma_start(out=scl[:, :], in_=scl_d).then_inc(sems["sld"], 16)

    em.op("g", const_loads)
    em.c["sld"] += 16 * 20
    N_CONST = em.val("sld")
    for eng in ("t", "v", "s"):
        em.wait_val(eng, "sld", N_CONST)

    wslot = [ws0, ws1]
    wpsem = ["swp0", "swp1"]
    xesem = ["sxe0", "sxe1"]
    pt = [pt0, pt1]
    obt = [obt0, obt1]
    shsem = ["ssh0", "ssh1"]
    xn_tok = [xn_tok0, xn_tok1]
    xasem = ["sxa0", "sxa1"]

    rel = {}

    def issue_pair_bundle(s, dc, wait_cur=False):
        slot = dc % 2
        if wait_cur:
            em.wait("g", "st")
        else:
            em.wait_val("g", "st", rel.get(("wp", s, dc), 0))
        sem = sems[wpsem[slot]]

        def f(g, dc=dc, slot=slot, sem=sem):
            sl = wslot[slot]
            g.dma_start(out=sl[:, 0:DM], in_=wqp_d[dc]).then_inc(sem, 16)
            g.dma_start(out=sl[:, DM:2 * DM], in_=wqs_d[dc]).then_inc(sem, 16)
            g.dma_start(out=sl[:, 2 * DM:3 * DM], in_=wkp_d[dc]).then_inc(sem, 16)
            g.dma_start(out=sl[:, 3 * DM:4 * DM], in_=wks_d[dc]).then_inc(sem, 16)
            g.dma_start(out=sl[:, 4 * DM:5 * DM], in_=wvn_d[dc]).then_inc(sem, 16)

        em.op("g", f)
        em.c[wpsem[slot]] += 80

    n_stores = 0
    for s in range(BC):
        if phases < 1:
            continue
        em.barrier()
        for tt in range(NT):
            slot = tt % 2
            axv = acc[:, slot * 1024:(slot + 1) * 1024]
            em.wait_val("g", "ss", rel.get(("xa", s, tt), 0))
            em.wait_val("g", "sst", rel.get(("accst",), 0))
            em.op("g", lambda g, axv=axv, tt=tt, s=s:
                  g.dma_start(out=axv, in_=x_d[s, tt * 128:(tt + 1) * 128, :]),
                  inc=xasem[slot], n=16)
            em.wait("v", xasem[slot])
            em.op("v", lambda v, axv=axv: v.bn_stats(st12[:, 0:6], axv[:, 0:512]))
            em.op("v", lambda v, axv=axv: v.bn_stats(st12[:, 6:12], axv[:, 512:1024]),
                  inc="sv")
            em.self_sync("v")
            em.op("v", lambda v: v.bn_aggr(mvar[:, :], st12[:, :]), inc="sv")
            em.self_sync("v")
            em.op("v", lambda v: v.tensor_scalar(lr1[:, :], mvar[:, 1:2], 1e-5, None, op0=ALU.add))
            em.op("v", lambda v: v.reciprocal(lr2[:, :], lr1[:, :]), inc="sv")
            em.wait("s", "sv")
            em.op("s", lambda sc: sc.activation(rstd[:, :], lr2[:, :], AF.Sqrt), inc="ss")
            em.wait("v", "ss")
            em.op("v", lambda v: v.tensor_scalar(nmr[:, :], mvar[:, 0:1], rstd[:, :], -1.0,
                                                 op0=ALU.mult, op1=ALU.mult), inc="sv")
            em.wait("s", "sv")
            em.op("s", lambda sc, axv=axv, slot=slot:
                  sc.activation(xn_tok[slot][:, :], axv, AI,
                                bias=nmr[:, :], scale=rstd[:, :]), inc="ss")
            rel[("xa", s, tt + 2)] = em.val("ss")
            bank = P[slot]
            em.wait("t", "ss")
            em.wait_val("t", "ss", rel.get(("xnev", s, tt), 0))
            for fc in range(NFC):
                em.op("t", lambda t, slot=slot, fc=fc, bank=bank:
                      t.transpose(bank[:, :].bitcast(BF16)[:, fc * 128:(fc + 1) * 128],
                                  xn_tok[slot][:, fc * 128:(fc + 1) * 128], idb[:, :]),
                      inc="st" if fc == NFC - 1 else None)
            em.wait("s", "st")
            for fc in range(NFC):
                em.op("s", lambda sc, fc=fc, tt=tt, bank=bank:
                      sc.activation(xnT[:, fc * L + tt * 128: fc * L + (tt + 1) * 128],
                                    bank[:, :].bitcast(BF16)[:, fc * 128:(fc + 1) * 128], AI),
                      inc="ss" if fc == NFC - 1 else None)
            rel[("xnev", s, tt + 2)] = em.val("ss")

        if phases < 2:
            continue
        em.barrier()
        for tt in range(NT):
            par = tt % 2
            bank = P[2 + 2 * par]
            gbank = P[3 + 2 * par]
            wb = wbfp[par]
            em.wait_val("t", "ss", rel.get(("gate", s, tt), 0))
            for fc in range(NFC):
                em.op("t", lambda t, fc=fc, tt=tt, bank=bank:
                      t.matmul(bank[:, 0:8], xnT[:, fc * L + tt * 128: fc * L + (tt + 1) * 128],
                               gw_sb[:, fc * 8:(fc + 1) * 8],
                               start=(fc == 0), stop=False))
            em.op("t", lambda t, bank=bank:
                  t.matmul(bank[:, 0:8], ones_r[0:1, 0:128], gb_sb[0:1, 0:8],
                           start=False, stop=True), inc="st")
            em.wait("v", "st")
            em.op("v", lambda v, bank=bank: v.reduce_max(gmx[:, :], bank[:, 0:8], axis=AX.X))
            em.op("v", lambda v: v.tensor_scalar(gmn[:, :], gmx[:, :], -1.0, None, op0=ALU.mult),
                  inc="sv")
            em.wait("s", "sv")
            em.op("s", lambda sc, bank=bank:
                  sc.activation(gex[:, :], bank[:, 0:8], AF.Exp, bias=gmn[:, :],
                                accum_out=gsm[:, :]), inc="ss")
            rel[("gate", s, tt + 2)] = em.val("ss")
            em.wait("v", "ss")
            em.op("v", lambda v: v.reciprocal(gr1[:, :], gsm[:, :]))
            em.op("v", lambda v: v.tensor_scalar(gn_[:, :], gex[:, :], gr1[:, :], None, op0=ALU.mult))
            em.op("v", lambda v: v.tensor_scalar(gmk[:, :], gn_[:, :], 0.1, None, op0=ALU.is_ge))
            em.op("v", lambda v: v.tensor_tensor(gw1[:, :], gn_[:, :], gmk[:, :], op=ALU.mult))
            em.op("v", lambda v: v.reduce_sum(gsm[:, :], gw1[:, :], axis=AX.X))
            em.op("v", lambda v: v.reciprocal(gr1[:, :], gsm[:, :]))
            em.op("v", lambda v, tt=tt: v.tensor_scalar(wcol[:, tt * 8:(tt + 1) * 8], gw1[:, :],
                                                        gr1[:, :], None, op0=ALU.mult))
            em.op("v", lambda v, wb=wb: v.tensor_scalar(wb[:, :], gw1[:, :], gr1[:, :], None,
                                                        op0=ALU.mult), inc="sv")
            em.wait("t", "sv")
            em.wait_val("t", "ss", rel.get(("wTev", s, tt), 0))
            em.op("t", lambda t, gbank=gbank, wb=wb:
                  t.transpose(gbank[:, :].bitcast(BF16)[0:8, 0:128],
                              wb[:, :], idb[:, :]), inc="st")
            em.wait("s", "st")
            em.op("s", lambda sc, tt=tt, gbank=gbank:
                  sc.activation(wT[0:8, tt * 128:(tt + 1) * 128],
                                gbank[:, :].bitcast(BF16)[0:8, 0:128], AI), inc="ss")
            rel[("wTev", s, tt + 2)] = em.val("ss")

        if phases < 3:
            continue
        for grp in range(2):
            em.barrier()
            em.wait_val("v", "sst", rel.get(("accst",), 0))
            if grp == 0:
                issue_pair_bundle(s, 0, wait_cur=True)
                issue_pair_bundle(s, 1, wait_cur=True)
            dcs = list(range(grp * GRP, grp * GRP + GRP))
            for j, dc in enumerate(dcs):
                slot = dc % 2
                sl = wslot[slot]
                em.wait("t", wpsem[slot])

                def proj(psum, woff, win, sl=sl, inc=None):
                    for fc in range(NFC):
                        em.op("t", lambda t, psum=psum, woff=woff, fc=fc, win=win, sl=sl:
                              t.matmul(psum[:, 0:512],
                                       sl[:, woff + fc * 128: woff + (fc + 1) * 128],
                                       xnT[:, fc * L + win * 512: fc * L + (win + 1) * 512],
                                       start=(fc == 0), stop=(fc == NFC - 1)),
                              inc=inc if fc == NFC - 1 else None)

                def emit_statq(pq):
                    sqv_p, j_p, win_p = pq
                    em.wait_val("t", "ss", sqv_p)
                    em.wait_val("t", "sv", rel.get(("kstatread",), 0))
                    stqv = em.op("t", lambda t:
                                 t.matmul(P[7][0:1, 0:512], ones_r[:, 0:1], qsqbQ[:, :],
                                          start=True, stop=True), inc="st")
                    rel[("stqv",)] = stqv
                    em.wait_val("v", "st", stqv)
                    if win_p == 0:
                        em.op("v", lambda v, j_p=j_p:
                              v.reduce_max(qmax[:, j_p:j_p + 1], P[7][0:1, 0:512], axis=AX.X),
                              inc="sv")
                    else:
                        em.op("v", lambda v: v.reduce_max(kmw[:, :], P[7][0:1, 0:512], axis=AX.X))
                        em.op("v", lambda v, j_p=j_p:
                              v.tensor_tensor(qmax[:, j_p:j_p + 1], qmax[:, j_p:j_p + 1],
                                              kmw[:, :], op=ALU.max), inc="sv")
                    rel[("statfree",)] = em.val("sv")

                for win in range(NWIN):
                    wsl = slice(win * 512, (win + 1) * 512)
                    em.wait_val("t", "sv", rel.get(("kfree",), 0))
                    proj(P[0], 2 * DM, win)
                    proj(P[1], 3 * DM, win, inc="st")
                    em.wait("v", "st")
                    em.op("v", lambda v, dc=dc, wsl=wsl:
                          v.scalar_tensor_tensor(t1b[:, :], P[0][:, 0:512], bkp[:, dc:dc + 1],
                                                 Ct[:, wsl], op0=ALU.add, op1=ALU.mult))
                    em.op("v", lambda v, dc=dc, wsl=wsl:
                          v.scalar_tensor_tensor(t2b[:, :], P[1][:, 0:512], bks[:, dc:dc + 1],
                                                 St[:, wsl], op0=ALU.add, op1=ALU.mult))
                    em.op("v", lambda v, j=j, wsl=wsl:
                          v.tensor_tensor(krotg(j)[:, wsl], t1b[:, :], t2b[:, :], op=ALU.add),
                          inc="sv")
                    rel[("kfree",)] = em.val("sv")
                    em.wait("s", "sv")
                    em.wait_val("s", "st", rel.get(("stkv",), 0))
                    skv = em.op("s", lambda sc, j=j, wsl=wsl:
                                sc.activation(qsqbK[:, :], krotg(j)[:, wsl], AF.Square),
                                inc="ss")
                    if rel.get(("pendq",)) is not None:
                        emit_statq(rel[("pendq",)])
                        rel[("pendq",)] = None
                    em.wait_val("t", "sv", rel.get(("vfree",), 0))
                    proj(P[6], 4 * DM, win, inc="st")
                    pvv = em.val("st")
                    em.wait_val("v", "st", pvv)
                    em.op("v", lambda v, dc=dc:
                          v.tensor_scalar(vtmp[:, :], P[6][:, 0:512], bvn[:, dc:dc + 1], None,
                                          op0=ALU.add), inc="sv")
                    vtmpv = em.val("sv")
                    rel[("vfree",)] = vtmpv
                    em.wait_val("t", "sv", rel.get(("qfree",), 0))
                    proj(P[2], 0, win)
                    proj(P[3], DM, win, inc="st")
                    em.wait("v", "st")
                    em.op("v", lambda v, dc=dc, wsl=wsl:
                          v.scalar_tensor_tensor(t1b[:, :], P[2][:, 0:512], bqp[:, dc:dc + 1],
                                                 Ct[:, wsl], op0=ALU.add, op1=ALU.mult))
                    em.op("v", lambda v, dc=dc, wsl=wsl:
                          v.scalar_tensor_tensor(t2b[:, :], P[3][:, 0:512], bqs[:, dc:dc + 1],
                                                 St[:, wsl], op0=ALU.add, op1=ALU.mult))
                    em.op("v", lambda v: v.tensor_tensor(t1b[:, :], t1b[:, :], t2b[:, :], op=ALU.add))
                    em.op("v", lambda v, j=j, wsl=wsl:
                          v.tensor_scalar(qrotg(j)[:, wsl], t1b[:, :], scl[:, :], None,
                                          op0=ALU.mult), inc="sv")
                    rel[("qfree",)] = em.val("sv")
                    em.wait("s", "sv")
                    em.wait_val("s", "st", rel.get(("stqv",), 0))
                    sqv = em.op("s", lambda sc, j=j, wsl=wsl:
                                sc.activation(qsqbQ[:, :], qrotg(j)[:, wsl], AF.Square),
                                inc="ss")
                    rel[("pendq",)] = (sqv, j, win)
                    em.wait_val("t", "ss", skv)
                    em.wait_val("t", "sv", rel.get(("statfree",), 0))
                    stkv = em.op("t", lambda t:
                                 t.matmul(P[7][0:1, 0:512], ones_r[:, 0:1], qsqbK[:, :],
                                          start=True, stop=True), inc="st")
                    rel[("stkv",)] = stkv
                    em.wait_val("v", "st", stkv)
                    if win == 0:
                        em.op("v", lambda v, j=j:
                              v.reduce_max(kmax[:, j:j + 1], P[7][0:1, 0:512], axis=AX.X),
                              inc="sv")
                    else:
                        em.op("v", lambda v: v.reduce_max(kmw[:, :], P[7][0:1, 0:512], axis=AX.X))
                        em.op("v", lambda v, j=j:
                              v.tensor_tensor(kmax[:, j:j + 1], kmax[:, j:j + 1], kmw[:, :],
                                              op=ALU.max), inc="sv")
                    rel[("kstatread",)] = em.val("sv")
                    em.wait_val("t", "sv", vtmpv)
                    em.wait_val("t", "ss", rel.get(("vtfree",), 0))
                    for h2 in range(2):
                        for jj in range(4):
                            em.op("t", lambda t, h2=h2, jj=jj:
                                  t.transpose(P[4 + h2][:, :].bitcast(BF16)[:, jj * 64:
                                                                            (jj + 1) * 64],
                                              vtmp[h2 * 64:(h2 + 1) * 64, jj * 128:(jj + 1) * 128],
                                              idb[h2 * 64:(h2 + 1) * 64, h2 * 64:(h2 + 1) * 64]),
                                  inc="st" if (h2 == 1 and jj == 3) else None)
                    em.wait("s", "st")
                    for h2 in range(2):
                        for jj in range(4):
                            kcg = win * 4 + jj
                            em.op("s", lambda sc, h2=h2, jj=jj, kcg=kcg, j=j:
                                  sc.activation(vhatg(j)[:, (h2 * 16 + kcg) * 65:
                                                         (h2 * 16 + kcg) * 65 + 64],
                                                P[4 + h2][:, :].bitcast(BF16)[:, jj * 64:
                                                                              (jj + 1) * 64],
                                                AI),
                                  inc="ss" if (h2 == 1 and jj == 3) else None)
                    rel[("vtfree",)] = em.val("ss")
                rel[("vhev", s, dc)] = em.val("ss")
                if dc + 2 < NP:
                    rel[("wp", s, dc + 2)] = em.val("st")
                    issue_pair_bundle(s, dc + 2)
            if phases < 3.5:
                continue
            if rel.get(("pendq",)) is not None:
                emit_statq(rel[("pendq",)])
                rel[("pendq",)] = None
            em.op("v", lambda v: v.tensor_tensor(mh2[:, :], qmax[:, :], kmax[:, :],
                                                 op=ALU.mult), inc="sv")
            em.wait("s", "sv")
            em.op("s", lambda sc: sc.activation(mh2[:, :], mh2[:, :], AF.Sqrt), inc="ss")
            em.wait("v", "ss")
            em.op("v", lambda v: v.tensor_scalar(mh2b[:, :], mh2[:, :], -1.0, None,
                                                 op0=ALU.mult), inc="sv")
            em.wait("t", "sv")
            em.op("t", lambda t: t.matmul(P[7][:, 0:GRP], ones_r[0:1, 0:128], mh2b[0:1, 0:GRP],
                                          start=True, stop=True), inc="st")
            em.wait("v", "st")
            em.op("v", lambda v: v.tensor_copy(negmh[:, :], P[7][:, 0:GRP]), inc="sv")
            nmv = em.val("sv")

            if phases < 4:
                continue
            SA = [S2[0][:, 0:512], S2[1][:, 0:512]]
            SB = [S2[0][:, 512:1024], S2[1][:, 512:1024]]
            em.wait("t", "ss")
            for j, dc in enumerate(dcs):
                em.wait_val("t", "ss", rel[("vhev", s, dc)])
                em.wait_val("t", "sv", nmv)
                for win in range(NWIN):
                    nkc = 4 * win + 4
                    for stage in range(nkc + 1):
                        if stage < nkc:
                            kc = stage
                            kb = kc % 2
                            off = max(0, (kc - 4 * win) * 128)
                            diag = kc >= 4 * win
                            qs0 = win * 512 + off
                            qs1 = (win + 1) * 512
                            em.wait_val("t", "ss", rel.get(("sexp", kb), 0))
                            for (Sp, base) in ((SA[kb], 0), (SB[kb], 64)):
                                lastg = (base == 64) and not diag
                                em.op("t", lambda t, Sp=Sp, base=base, kc=kc, off=off, qs0=qs0,
                                            qs1=qs1, diag=diag, j=j:
                                      t.matmul(Sp[:, off:512],
                                               krotg(j)[base:base + 64, kc * 128:(kc + 1) * 128],
                                               qrotg(j)[base:base + 64, qs0:qs1],
                                               start=True, stop=not diag),
                                      inc="st" if lastg else None)
                                if diag:
                                    em.op("t", lambda t, Sp=Sp, off=off:
                                          t.matmul(Sp[:, off:off + 128], idb[:, :], tri[:, :],
                                                   start=False, stop=True),
                                          inc="st" if base == 64 else None)
                            em.wait("s", "st")
                            em.op("s", lambda sc, kb=kb, off=off, j=j:
                                  sc.activation(pt[kb][:, off:1024], S2[kb][:, off:1024],
                                                AF.Exp, bias=negmh[:, j:j + 1]), inc="ss")
                            rel[("sexp", kb)] = em.val("ss")
                            rel[("expv", j, kc)] = em.val("ss")
                        if stage >= 1:
                            kc = stage - 1
                            kb = kc % 2
                            off = max(0, (kc - 4 * win) * 128)
                            em.wait_val("t", "ss", rel[("expv", j, kc)])
                            em.op("t", lambda t, kb=kb, kc=kc, off=off, j=j,
                                        last=(kc == nkc - 1):
                                  t.matmul(P[6][0:65, off:512],
                                           vhatg(j)[:, kc * 65: kc * 65 + 65],
                                           pt[kb][:, off:512], start=(kc == 0), stop=last))
                            em.op("t", lambda t, kb=kb, kc=kc, off=off, j=j,
                                        last=(kc == nkc - 1):
                                  t.matmul(P[7][0:65, off:512],
                                           vhatg(j)[:, (16 + kc) * 65: (16 + kc) * 65 + 65],
                                           pt[kb][:, 512 + off:1024], start=(kc == 0), stop=last),
                                  inc="st")
                    em.wait("s", "st")
                    em.wait_val("s", shsem[win % 2], 16 * rel.get(("shcnt", win % 2), 0))
                    em.wait_val("s", "ssum", 32 * rel.get(("sumcnt",), 0))
                    em.op("s", lambda sc, win=win:
                          sc.activation(s64[64:65, win * 512:(win + 1) * 512],
                                        P[6][64:65, 0:512], AI))
                    em.op("s", lambda sc, win=win:
                          sc.activation(s64[64:65, 2048 + win * 512: 2048 + (win + 1) * 512],
                                        P[7][64:65, 0:512], AI))
                    em.op("s", lambda sc, dc=dc, win=win:
                          sc.activation(onT[0:64, dc * L + win * 512: dc * L + (win + 1) * 512],
                                        P[6][0:64, 0:512], AI))
                    em.op("s", lambda sc, win=win:
                          sc.activation(obt[win % 2][:, :], P[7][0:64, 0:512], AI), inc="ss")
                    em.wait("g", "ss")
                    em.op("g", lambda g, dc=dc, win=win:
                          g.dma_start(out=onT[64:128, dc * L + win * 512: dc * L + (win + 1) * 512],
                                      in_=obt[win % 2][:, :]), inc=shsem[win % 2], n=16)
                    rel[("shcnt", win % 2)] = rel.get(("shcnt", win % 2), 0) + 1
                em.wait_val("g", "sv", rel.get(("sumfree",), 0))
                em.op("g", lambda g, dc=dc:
                      g.dma_start(out=sums[2 * dc:2 * dc + 1, 0:L], in_=s64[64:65, 0:L]),
                      inc="ssum", n=16)
                em.op("g", lambda g, dc=dc:
                      g.dma_start(out=sums[2 * dc + 1:2 * dc + 2, 0:L], in_=s64[64:65, L:2 * L]),
                      inc="ssum", n=16)
                rel[("sumcnt",)] = rel.get(("sumcnt",), 0) + 1

        if phases < 5:
            continue
        em.barrier()
        em.wait("v", "ssh0")
        em.wait("v", "ssh1")
        em.wait("v", "ssum")
        def _recip(v, nc=nc):
            with nc.allow_low_precision(reason="recip in f32 internally, one rounding"):
                return v.reciprocal(rsbf[:, :], sums[:, :])

        em.op("v", _recip, inc="sv")
        rel[("sumfree",)] = em.val("sv")
        rsbase = [0, 32]
        for dc in range(NP):
            em.wait("g", "sv")
            em.wait_val("g", "st", rel.get(("rsfree", dc % 2), 0))
            rb = rsbase[dc % 2]
            em.op("g", lambda g, dc=dc, rb=rb: g.dma_start(out=s64[rb:rb + 1, 0:L],
                                                           in_=rsbf[2 * dc:2 * dc + 1, 0:L]),
                  inc="srs", n=16)
            rsw = em.op("g", lambda g, dc=dc, rb=rb:
                        g.dma_start(out=s64[rb:rb + 1, L:2 * L],
                                    in_=rsbf[2 * dc + 1:2 * dc + 2, 0:L]),
                        inc="srs", n=16)
            em.wait_val("t", "srs", rsw)
            for win in range(NWIN):
                par = (dc * NWIN + win) % 2
                bank = P[2 + par]
                em.wait_val("t", "sv", rel.get(("obc", par), 0))
                em.op("t", lambda t, win=win, bank=bank, rb=rb:
                      t.matmul(bank[0:64, 0:512], ones_r[rb:rb + 1, 0:64],
                               s64[rb:rb + 1, win * 512:(win + 1) * 512],
                               start=True, stop=True))
                em.op("t", lambda t, win=win, bank=bank, rb=rb:
                      t.matmul(bank[64:128, 0:512], ones_r[rb:rb + 1, 0:64],
                               s64[rb:rb + 1, L + win * 512: L + (win + 1) * 512],
                               start=True, stop=True), inc="st")
                if win == NWIN - 1:
                    rel[("rsfree", dc % 2)] = em.val("st")
                em.wait("v", "st")
                em.op("v", lambda v, dc=dc, win=win, bank=bank:
                      v.tensor_tensor(onT[:, dc * L + win * 512: dc * L + (win + 1) * 512],
                                      onT[:, dc * L + win * 512: dc * L + (win + 1) * 512],
                                      bank[:, 0:512], op=ALU.mult), inc="sv")
                rel[("obc", par)] = em.val("sv")

        if phases < 6:
            continue
        em.barrier()
        for icol in range(2):
            em.wait_val("g", "sst", rel.get(("accst",), 0))
            em.wait("g", "sv")
            for tt in range(NT):
                em.op("g", lambda g, s=s, tt=tt, icol=icol:
                      g.dma_start(out=acc[:, tt * 512:(tt + 1) * 512],
                                  in_=x_d[s, tt * 128:(tt + 1) * 128,
                                          icol * 512:(icol + 1) * 512]),
                      inc="sxf", n=16)
            xe_wait = {}
            st0 = em.val("st")
            for e in range(9):
                slot = e % 2
                em.wait_val("g", "st", st0 if e < 2 else st0 + (e - 1) * 16)
                xe_wait[e] = em.op("g", lambda g, e=e, icol=icol, slot=slot:
                                   g.dma_start(out=wslot[slot][:, 0:NFC * 512],
                                               in_=wex_d[e, icol]),
                                   inc=xesem[slot], n=16)
            for e in range(9):
                slot = e % 2
                em.wait_val("t", xesem[slot], xe_wait[e])
                for tt in range(NT):
                    par = tt % 2
                    bank = P[par]
                    em.wait_val("t", "sv", rel.get(("fps", par), 0))
                    src = xnT if e < 8 else onT
                    for fc in range(NFC):
                        em.op("t", lambda t, bank=bank, fc=fc, tt=tt, slot=slot, src=src, e=e:
                              t.matmul(bank[:, 0:512],
                                       src[:, fc * L + tt * 128: fc * L + (tt + 1) * 128],
                                       wslot[slot][:, fc * 512:(fc + 1) * 512],
                                       start=(fc == 0), stop=(e < 8 and fc == NFC - 1)),
                              inc="st" if (e < 8 and fc == NFC - 1) else None)
                    if e == 8:
                        em.op("t", lambda t, bank=bank, icol=icol:
                              t.matmul(bank[:, 0:512], ones_r[0:1, 0:128],
                                       bo_sb[0:1, icol * 512:(icol + 1) * 512],
                                       start=False, stop=False))
                        em.op("t", lambda t, bank=bank, tt=tt, icol=icol:
                              t.matmul(bank[:, 0:512], wT[0:8, tt * 128:(tt + 1) * 128],
                                       eb_sb[0:8, icol * 512:(icol + 1) * 512],
                                       start=False, stop=True), inc="st")
                    em.wait("v", "st")
                    if e == 0:
                        em.wait("v", "sxf")
                        em.wait_val("v", "sst", rel.get(("accst",), 0))
                    asl = acc[:, tt * 512:(tt + 1) * 512]
                    if e < 8:
                        em.op("v", lambda v, bank=bank, tt=tt, e=e, asl=asl:
                              v.scalar_tensor_tensor(asl, bank[:, 0:512],
                                                     wcol[:, tt * 8 + e: tt * 8 + e + 1],
                                                     asl, op0=ALU.mult, op1=ALU.add), inc="sv")
                    else:
                        em.op("v", lambda v, bank=bank, asl=asl:
                              v.tensor_tensor(asl, asl, bank[:, 0:512], op=ALU.add), inc="sv")
                    rel[("fps", par)] = em.val("sv")
                    if e == 8:
                        em.wait("y", "sv")
                        em.op("y", lambda y, s=s, tt=tt, icol=icol:
                              y.dma_start(out=out_d[s, tt * 128:(tt + 1) * 128,
                                                    icol * 512:(icol + 1) * 512],
                                          in_=acc[:, tt * 512:(tt + 1) * 512]),
                              inc="sst", n=16)
                        n_stores += 1
            rel[("accst",)] = em.val("sst")

    em.wait_val("y", "sst", 16 * n_stores)

    with nc.Block() as block:
        @block.tensor
        def _(t):
            for f in em.L["t"]:
                f(t)

        @block.vector
        def _(v):
            for f in em.L["v"]:
                f(v)

        @block.scalar
        def _(sc):
            for f in em.L["s"]:
                f(sc)

        @block.gpsimd
        def _(g):
            for f in em.L["g"]:
                f(g)

        @block.sync
        def _(y):
            for f in em.L["y"]:
                f(y)

    es.close()
    return nc


def _host_inputs(L0_attn_Wq, L0_attn_bq, L0_attn_Wk, L0_attn_bk, L0_attn_Wv,
                 L0_attn_bv, L0_attn_Wo, L0_attn_bo, L0_attn_scale,
                 L0_ffn_gate_w, L0_ffn_gate_b, L0_ffn_expert_w, L0_ffn_expert_b):
    f32 = np.float32
    Wq = np.asarray(L0_attn_Wq, f32)
    Wk = np.asarray(L0_attn_Wk, f32)
    Wv = np.asarray(L0_attn_Wv, f32)
    Wo = np.asarray(L0_attn_Wo, f32)
    bq = np.asarray(L0_attn_bq, f32)
    bk = np.asarray(L0_attn_bk, f32)
    bv = np.asarray(L0_attn_bv, f32)
    bo = np.asarray(L0_attn_bo, f32)
    scale = np.asarray(L0_attn_scale, f32)
    gw = np.asarray(L0_ffn_gate_w, f32)
    gb = np.asarray(L0_ffn_gate_b, f32)
    ew = np.asarray(L0_ffn_expert_w, f32)
    eb = np.asarray(L0_ffn_expert_b, f32)

    pi = np.zeros(DM, np.int64)
    sig = np.zeros(DM, np.int64)
    for n in range(NH):
        b0 = n * HD
        pi[b0:b0 + 32] = b0 + 2 * np.arange(32)
        pi[b0 + 32:b0 + 64] = b0 + 2 * np.arange(32) + 1
        sig[b0 + 2 * np.arange(32)] = b0 + 2 * np.arange(32) + 1
        sig[b0 + 2 * np.arange(32) + 1] = b0 + 2 * np.arange(32)

    def pair_img(WT):
        return np.ascontiguousarray(
            WT.reshape(8, 128, 8, 128).transpose(2, 1, 0, 3).reshape(8, 128, 1024))

    ins = {}
    ins["wqp"] = pair_img(Wq.T[:, pi])
    ins["wqs"] = pair_img(Wq.T[:, sig[pi]])
    ins["wkp"] = pair_img(Wk.T[:, pi])
    ins["wks"] = pair_img(Wk.T[:, sig[pi]])
    ins["wvn"] = pair_img(Wv.T)
    wex = np.zeros((9, 2, 128, NFC * 512), f32)
    for e in range(8):
        wex[e] = ew[e].reshape(8, 128, 2, 512).transpose(2, 1, 0, 3).reshape(2, 128, NFC * 512)
    wex[8] = Wo.T.reshape(8, 128, 2, 512).transpose(2, 1, 0, 3).reshape(2, 128, NFC * 512)
    ins["wex"] = np.ascontiguousarray(wex)
    ins["gw"] = np.ascontiguousarray(gw.reshape(8, 128, 8).transpose(1, 0, 2).reshape(128, 64))
    ins["bqp"] = np.ascontiguousarray(bq[pi].reshape(8, 128).T)
    ins["bqs"] = np.ascontiguousarray(bq[sig[pi]].reshape(8, 128).T)
    ins["bkp"] = np.ascontiguousarray(bk[pi].reshape(8, 128).T)
    ins["bks"] = np.ascontiguousarray(bk[sig[pi]].reshape(8, 128).T)
    ins["bvn"] = np.ascontiguousarray(bv.reshape(8, 128).T)
    ins["bo"] = bo.reshape(1, DM)
    ins["gb"] = gb.reshape(1, 8)
    ins["eb"] = eb
    ins["scl"] = np.repeat(scale.reshape(1, 1), 128, axis=0)

    freqs = 10000.0 ** (-np.arange(0, HD, 2, dtype=np.float64) / HD)
    ang = np.arange(L, dtype=np.float64)[:, None] * freqs[None, :]
    cosT = np.cos(ang).astype(f32).T
    sinT = np.sin(ang).astype(f32).T
    C64 = np.concatenate([cosT, cosT], 0)
    S64 = np.concatenate([-sinT, sinT], 0)
    ins["ct"] = np.ascontiguousarray(np.concatenate([C64, C64], 0))
    ins["stb"] = np.ascontiguousarray(np.concatenate([S64, S64], 0))

    k_i = np.arange(128)
    ins["tri"] = np.where(k_i[None, :] < k_i[:, None], f32(-65536.0), f32(0.0))
    ins["idn"] = np.eye(128, dtype=f32)
    e2c = np.zeros((128, 2), f32)
    e2c[0:64, 0] = 1.0
    e2c[64:128, 1] = 1.0
    ins["e2"] = e2c
    ins["ones"] = np.ones((128, 512), f32)
    vini = np.zeros((128, 2 * 16 * 65), f32)
    vini[:, 64::65] = 1.0
    ins["vini"] = vini
    return ins


def kernel(L0_attn_x, L0_attn_Wq, L0_attn_bq, L0_attn_Wk, L0_attn_bk,
           L0_attn_Wv, L0_attn_bv, L0_attn_Wo, L0_attn_bo, L0_attn_scale,
           L0_ffn_gate_w, L0_ffn_gate_b, L0_ffn_expert_w, L0_ffn_expert_b):
    x = np.ascontiguousarray(np.asarray(L0_attn_x, np.float32))
    shared = _host_inputs(L0_attn_Wq, L0_attn_bq, L0_attn_Wk, L0_attn_bk,
                          L0_attn_Wv, L0_attn_bv, L0_attn_Wo, L0_attn_bo,
                          L0_attn_scale, L0_ffn_gate_w, L0_ffn_gate_b,
                          L0_ffn_expert_w, L0_ffn_expert_b)
    nc = build_program()
    in_maps = []
    for c in range(8):
        m = dict(shared)
        m["x"] = x[2 * c:2 * c + 2]
        in_maps.append(m)
    res = run_bass_kernel_spmd(nc, in_maps, core_ids=list(range(8)))
    out = np.concatenate([res.results[c]["out"] for c in range(8)], axis=0)
    return out


# version 12
# speedup vs baseline: 1.0634x; 1.0040x over previous
import numpy as np
from contextlib import ExitStack

import concourse.bass as bass
import concourse.mybir as mybir
from concourse.bass_utils import run_bass_kernel_spmd

F32 = mybir.dt.float32
BF16 = mybir.dt.bfloat16
AF = mybir.ActivationFunctionType
ALU = mybir.AluOpType
AX = mybir.AxisListType

BC = 2
L = 2048
DM = 1024
NH = 16
HD = 64
NP = 8
NWIN = 4
NT = 16
NFC = 8
GRP = 4


class Em:

    ENGS = ("t", "v", "s", "g", "y")
    OWN = {"t": "st", "v": "sv", "s": "ss"}

    def __init__(self, sems):
        self.L = {k: [] for k in self.ENGS}
        self.sems = sems
        self.c = {k: 0 for k in sems}
        self.lastwait = {}

    def val(self, name):
        return self.c[name]

    def bump(self, name, n=1):
        self.c[name] += n
        return self.c[name]

    def wait_val(self, eng, name, v):
        if v <= 0:
            return
        if self.OWN.get(eng) == name:
            return
        key = (eng, name)
        if self.lastwait.get(key, 0) >= v:
            return
        self.lastwait[key] = v
        sem = self.sems[name]
        self.L[eng].append(lambda e, sem=sem, v=v: e.wait_ge(sem, v))

    def wait(self, eng, name):
        self.wait_val(eng, name, self.val(name))

    def op(self, eng, fn, inc=None, n=1):
        if eng == "v":
            own = self.OWN[eng]
            pv = self.val(own)
            sem0 = self.sems[own]
            if pv > 0:
                self.L[eng].append(lambda e, sem0=sem0, pv=pv: e.wait_ge(sem0, pv))
            inc = own
            n = 1
        if inc is None:
            self.L[eng].append(fn)
            return None
        v = self.bump(inc, n)
        sem = self.sems[inc]
        self.L[eng].append(lambda e, fn=fn, sem=sem, n=n: fn(e).then_inc(sem, n))
        return v

    def self_sync(self, eng):
        name = self.OWN[eng]
        v = self.val(name)
        sem = self.sems[name]
        self.L[eng].append(lambda e, sem=sem, v=v: e.wait_ge(sem, v))

    def barrier(self, engs=("t", "v", "s")):
        vals = {k: self.val(v) for k, v in self.OWN.items()}
        for eng in engs:
            for src, name in self.OWN.items():
                if src != eng:
                    self.wait_val(eng, name, vals[src])


def build_program(phases=99.0):
    nc = bass.Bass("TRN2", target_bir_lowering=False, detect_race_conditions=False)

    def din(name, shape):
        return nc.dram_tensor(name, list(shape), F32, kind="ExternalInput").ap()

    x_d = din("x", (BC, L, DM))
    wqp_d = din("wqp", (NP, 128, DM))
    wqs_d = din("wqs", (NP, 128, DM))
    wkp_d = din("wkp", (NP, 128, DM))
    wks_d = din("wks", (NP, 128, DM))
    wvn_d = din("wvn", (NP, 128, DM))
    wex_d = din("wex", (9, 2, 128, NFC * 512))
    gw_d = din("gw", (128, 64))
    bqp_d = din("bqp", (128, 8))
    bqs_d = din("bqs", (128, 8))
    bkp_d = din("bkp", (128, 8))
    bks_d = din("bks", (128, 8))
    bvn_d = din("bvn", (128, 8))
    bo_d = din("bo", (1, DM))
    gb_d = din("gb", (1, 8))
    eb_d = din("eb", (8, DM))
    ct_d = din("ct", (128, L))
    st_d = din("stb", (128, L))
    tri_d = din("tri", (128, 128))
    idn_d = din("idn", (128, 128))
    e2_d = din("e2", (128, 2))
    ones_d = din("ones", (128, 512))
    vini_d = din("vini", (128, 2 * 16 * 65))
    scl_d = din("scl", (128, 1))

    out_d = nc.dram_tensor("out", [BC, L, DM], F32, kind="ExternalOutput").ap()

    es = ExitStack()

    def sb(name, shape, dt):
        return es.enter_context(nc.sbuf_tensor("s_" + name, list(shape), dt))

    def ps(name):
        return es.enter_context(nc.psum_tensor(name, [128, 512], F32))

    xnT = sb("xnT", (128, NFC * L), BF16)
    onT = sb("onT", (128, NP * L), BF16)
    UB = sb("UB", (128, (2 * GRP * L + GRP * 2 * 16 * 65) // 2), F32)
    UBb = UB[:, :].bitcast(BF16)

    def qrotg(j):
        return UBb[:, j * L:(j + 1) * L]

    def krotg(j):
        return UBb[:, GRP * L + j * L: GRP * L + (j + 1) * L]

    def vhatg(j):
        return UBb[:, 2 * GRP * L + j * 2080: 2 * GRP * L + (j + 1) * 2080]

    acc = UB
    sums = sb("sums", (16, L), BF16)
    rsbf = sb("rsbf", (16, L), BF16)
    s64 = sb("s64", (128, 4096), BF16)
    Ct = sb("Ct", (128, L), F32)
    St = sb("St", (128, L), F32)
    tri = sb("tri_s", (128, 128), BF16)
    idb = sb("idb", (128, 128), BF16)
    e2 = sb("e2_s", (128, 2), BF16)
    ones_r = sb("ones_r", (128, 512), BF16)
    bqp = sb("bqp", (128, 8), F32)
    bqs = sb("bqs", (128, 8), F32)
    bkp = sb("bkp", (128, 8), F32)
    bks = sb("bks", (128, 8), F32)
    bvn = sb("bvn", (128, 8), F32)
    bo_sb = sb("bo_sb", (1, DM), BF16)
    gb_sb = sb("gb_sb", (1, 8), BF16)
    eb_sb = sb("eb_sb", (8, DM), BF16)
    gw_sb = sb("gw_sb", (128, 64), BF16)
    scl = sb("scl_sb", (128, 1), F32)
    ws0 = sb("ws0", (128, 5 * DM), BF16)
    ws1 = sb("ws1", (128, 5 * DM), BF16)
    pt0 = sb("pt0", (128, 1024), BF16)
    pt1 = sb("pt1", (128, 1024), BF16)
    t1b = sb("t1b", (128, 512), F32)
    t2b = sb("t2b", (128, 512), F32)
    vtmp = sb("vtmp", (128, 512), BF16)
    obt0 = sb("obt0", (64, 512), BF16)
    obt1 = sb("obt1", (64, 512), BF16)
    xn_tok0 = sb("xn_tok0", (128, DM), BF16)
    xn_tok1 = sb("xn_tok1", (128, DM), BF16)
    st12 = sb("st12", (128, 12), F32)
    mvar = sb("mvar", (128, 2), F32)
    lr1 = sb("lr1", (128, 1), F32)
    lr2 = sb("lr2", (128, 1), F32)
    rstd = sb("rstd", (128, 1), F32)
    nmr = sb("nmr", (128, 1), F32)
    gex = sb("gex", (128, 8), F32)
    gmx = sb("gmx", (128, 1), F32)
    gmn = sb("gmn", (128, 1), F32)
    gsm = sb("gsm", (128, 1), F32)
    gr1 = sb("gr1", (128, 1), F32)
    gn_ = sb("gn", (128, 8), F32)
    gmk = sb("gmk", (128, 8), F32)
    gw1 = sb("gw1", (128, 8), F32)
    wcol = sb("wcol", (128, NT * 8), F32)
    wbf = sb("wbf", (128, 8), BF16)
    wbf1 = sb("wbf1", (128, 8), BF16)
    wbfp = [wbf, wbf1]
    wT = sb("wT", (8, NT * 128), BF16)
    qmax = sb("qmax", (1, GRP), F32)
    kmax = sb("kmax", (1, GRP), F32)
    kmw = sb("kmw", (1, 1), F32)
    mh2 = sb("mh2", (1, GRP), F32)
    mh2b = sb("mh2b", (1, GRP), BF16)
    negmh = sb("negmh", (128, GRP), F32)

    qsqbK = pt0[:, 0:512]
    qsqbQ = pt0[:, 512:1024]

    S2 = [es.enter_context(nc.psum_tensor("S2_%d" % i, [128, 1024], F32))
          for i in range(2)]
    P = [ps("P0"), ps("P1"),
         S2[0][:, 0:512], S2[0][:, 512:1024],
         S2[1][:, 0:512], S2[1][:, 512:1024],
         ps("P6"), ps("P7")]

    sem_names = ["st", "sv", "ss", "sld", "swp0", "swp1", "sxe0", "sxe1",
                 "sxf", "ssh0", "ssh1", "sst", "sxa0", "sxa1", "ssum", "srs"]
    sems = {n: es.enter_context(nc.semaphore(n)) for n in sem_names}
    em = Em(sems)

    AI = AF.Identity

    def const_loads(g):
        g.dma_start(out=Ct[:, :], in_=ct_d).then_inc(sems["sld"], 16)
        g.dma_start(out=St[:, :], in_=st_d).then_inc(sems["sld"], 16)
        g.dma_start(out=tri[:, :], in_=tri_d).then_inc(sems["sld"], 16)
        g.dma_start(out=idb[:, :], in_=idn_d).then_inc(sems["sld"], 16)
        g.dma_start(out=e2[:, :], in_=e2_d).then_inc(sems["sld"], 16)
        g.dma_start(out=ones_r[:, :], in_=ones_d).then_inc(sems["sld"], 16)
        for dst, src in ((bqp, bqp_d), (bqs, bqs_d), (bkp, bkp_d),
                         (bks, bks_d), (bvn, bvn_d)):
            g.dma_start(out=dst[:, :], in_=src).then_inc(sems["sld"], 16)
        g.dma_start(out=bo_sb[:, :], in_=bo_d).then_inc(sems["sld"], 16)
        g.dma_start(out=gb_sb[:, :], in_=gb_d).then_inc(sems["sld"], 16)
        g.dma_start(out=eb_sb[:, :], in_=eb_d).then_inc(sems["sld"], 16)
        g.dma_start(out=gw_sb[:, :], in_=gw_d).then_inc(sems["sld"], 16)
        for j in range(GRP):
            g.dma_start(out=vhatg(j)[:, :], in_=vini_d).then_inc(sems["sld"], 16)
        g.dma_start(out=scl[:, :], in_=scl_d).then_inc(sems["sld"], 16)

    em.op("g", const_loads)
    em.c["sld"] += 16 * 20
    N_CONST = em.val("sld")
    for eng in ("t", "v", "s"):
        em.wait_val(eng, "sld", N_CONST)

    wslot = [ws0, ws1]
    wpsem = ["swp0", "swp1"]
    xesem = ["sxe0", "sxe1"]
    pt = [pt0, pt1]
    obt = [obt0, obt1]
    shsem = ["ssh0", "ssh1"]
    xn_tok = [xn_tok0, xn_tok1]
    xasem = ["sxa0", "sxa1"]

    rel = {}

    def issue_pair_bundle(s, dc, wait_cur=False):
        slot = dc % 2
        if wait_cur:
            em.wait("g", "st")
        else:
            em.wait_val("g", "st", rel.get(("wp", s, dc), 0))
        sem = sems[wpsem[slot]]

        def f(g, dc=dc, slot=slot, sem=sem):
            sl = wslot[slot]
            g.dma_start(out=sl[:, 0:DM], in_=wqp_d[dc]).then_inc(sem, 16)
            g.dma_start(out=sl[:, DM:2 * DM], in_=wqs_d[dc]).then_inc(sem, 16)
            g.dma_start(out=sl[:, 2 * DM:3 * DM], in_=wkp_d[dc]).then_inc(sem, 16)
            g.dma_start(out=sl[:, 3 * DM:4 * DM], in_=wks_d[dc]).then_inc(sem, 16)
            g.dma_start(out=sl[:, 4 * DM:5 * DM], in_=wvn_d[dc]).then_inc(sem, 16)

        em.op("g", f)
        em.c[wpsem[slot]] += 80

    n_stores = 0
    for s in range(BC):
        if phases < 1:
            continue
        em.barrier()
        for tt in range(NT):
            slot = tt % 2
            axv = acc[:, slot * 1024:(slot + 1) * 1024]
            em.wait_val("g", "ss", rel.get(("xa", s, tt), 0))
            em.wait_val("g", "sst", rel.get(("accst",), 0))
            em.op("g", lambda g, axv=axv, tt=tt, s=s:
                  g.dma_start(out=axv, in_=x_d[s, tt * 128:(tt + 1) * 128, :]),
                  inc=xasem[slot], n=16)
            em.wait("v", xasem[slot])
            em.op("v", lambda v, axv=axv: v.bn_stats(st12[:, 0:6], axv[:, 0:512]))
            em.op("v", lambda v, axv=axv: v.bn_stats(st12[:, 6:12], axv[:, 512:1024]),
                  inc="sv")
            em.self_sync("v")
            em.op("v", lambda v: v.bn_aggr(mvar[:, :], st12[:, :]), inc="sv")
            em.self_sync("v")
            em.op("v", lambda v: v.tensor_scalar(lr1[:, :], mvar[:, 1:2], 1e-5, None, op0=ALU.add))
            em.op("v", lambda v: v.reciprocal(lr2[:, :], lr1[:, :]), inc="sv")
            em.wait("s", "sv")
            em.op("s", lambda sc: sc.activation(rstd[:, :], lr2[:, :], AF.Sqrt), inc="ss")
            em.wait("v", "ss")
            em.op("v", lambda v: v.tensor_scalar(nmr[:, :], mvar[:, 0:1], rstd[:, :], -1.0,
                                                 op0=ALU.mult, op1=ALU.mult), inc="sv")
            em.wait("s", "sv")
            em.op("s", lambda sc, axv=axv, slot=slot:
                  sc.activation(xn_tok[slot][:, :], axv, AI,
                                bias=nmr[:, :], scale=rstd[:, :]), inc="ss")
            rel[("xa", s, tt + 2)] = em.val("ss")
            bank = P[slot]
            em.wait("t", "ss")
            em.wait_val("t", "ss", rel.get(("xnev", s, tt), 0))
            for fc in range(NFC):
                em.op("t", lambda t, slot=slot, fc=fc, bank=bank:
                      t.transpose(bank[:, :].bitcast(BF16)[:, fc * 128:(fc + 1) * 128],
                                  xn_tok[slot][:, fc * 128:(fc + 1) * 128], idb[:, :]),
                      inc="st" if fc == NFC - 1 else None)
            em.wait("s", "st")
            for fc in range(NFC):
                em.op("s", lambda sc, fc=fc, tt=tt, bank=bank:
                      sc.activation(xnT[:, fc * L + tt * 128: fc * L + (tt + 1) * 128],
                                    bank[:, :].bitcast(BF16)[:, fc * 128:(fc + 1) * 128], AI),
                      inc="ss" if fc == NFC - 1 else None)
            rel[("xnev", s, tt + 2)] = em.val("ss")

        if phases < 2:
            continue
        em.barrier()
        for tt in range(NT):
            par = tt % 2
            bank = P[2 + 2 * par]
            gbank = P[3 + 2 * par]
            wb = wbfp[par]
            em.wait_val("t", "ss", rel.get(("gate", s, tt), 0))
            for fc in range(NFC):
                em.op("t", lambda t, fc=fc, tt=tt, bank=bank:
                      t.matmul(bank[:, 0:8], xnT[:, fc * L + tt * 128: fc * L + (tt + 1) * 128],
                               gw_sb[:, fc * 8:(fc + 1) * 8],
                               start=(fc == 0), stop=False))
            em.op("t", lambda t, bank=bank:
                  t.matmul(bank[:, 0:8], ones_r[0:1, 0:128], gb_sb[0:1, 0:8],
                           start=False, stop=True), inc="st")
            em.wait("v", "st")
            em.op("v", lambda v, bank=bank: v.reduce_max(gmx[:, :], bank[:, 0:8], axis=AX.X))
            em.op("v", lambda v: v.tensor_scalar(gmn[:, :], gmx[:, :], -1.0, None, op0=ALU.mult),
                  inc="sv")
            em.wait("s", "sv")
            em.op("s", lambda sc, bank=bank:
                  sc.activation(gex[:, :], bank[:, 0:8], AF.Exp, bias=gmn[:, :],
                                accum_out=gsm[:, :]), inc="ss")
            rel[("gate", s, tt + 2)] = em.val("ss")
            em.wait("v", "ss")
            em.op("v", lambda v: v.reciprocal(gr1[:, :], gsm[:, :]))
            em.op("v", lambda v: v.tensor_scalar(gn_[:, :], gex[:, :], gr1[:, :], None, op0=ALU.mult))
            em.op("v", lambda v: v.tensor_scalar(gmk[:, :], gn_[:, :], 0.1, None, op0=ALU.is_ge))
            em.op("v", lambda v: v.tensor_tensor(gw1[:, :], gn_[:, :], gmk[:, :], op=ALU.mult))
            em.op("v", lambda v: v.reduce_sum(gsm[:, :], gw1[:, :], axis=AX.X))
            em.op("v", lambda v: v.reciprocal(gr1[:, :], gsm[:, :]))
            em.op("v", lambda v, tt=tt: v.tensor_scalar(wcol[:, tt * 8:(tt + 1) * 8], gw1[:, :],
                                                        gr1[:, :], None, op0=ALU.mult))
            em.op("v", lambda v, wb=wb: v.tensor_scalar(wb[:, :], gw1[:, :], gr1[:, :], None,
                                                        op0=ALU.mult), inc="sv")
            em.wait("t", "sv")
            em.wait_val("t", "ss", rel.get(("wTev", s, tt), 0))
            em.op("t", lambda t, gbank=gbank, wb=wb:
                  t.transpose(gbank[:, :].bitcast(BF16)[0:8, 0:128],
                              wb[:, :], idb[:, :]), inc="st")
            em.wait("s", "st")
            em.op("s", lambda sc, tt=tt, gbank=gbank:
                  sc.activation(wT[0:8, tt * 128:(tt + 1) * 128],
                                gbank[:, :].bitcast(BF16)[0:8, 0:128], AI), inc="ss")
            rel[("wTev", s, tt + 2)] = em.val("ss")

        if phases < 3:
            continue
        for grp in range(2):
            em.barrier()
            em.wait_val("v", "sst", rel.get(("accst",), 0))
            if grp == 0:
                issue_pair_bundle(s, 0, wait_cur=True)
                issue_pair_bundle(s, 1, wait_cur=True)
            dcs = list(range(grp * GRP, grp * GRP + GRP))
            for j, dc in enumerate(dcs):
                slot = dc % 2
                sl = wslot[slot]
                em.wait("t", wpsem[slot])

                def proj(psum, woff, win, sl=sl, inc=None):
                    for fc in range(NFC):
                        em.op("t", lambda t, psum=psum, woff=woff, fc=fc, win=win, sl=sl:
                              t.matmul(psum[:, 0:512],
                                       sl[:, woff + fc * 128: woff + (fc + 1) * 128],
                                       xnT[:, fc * L + win * 512: fc * L + (win + 1) * 512],
                                       start=(fc == 0), stop=(fc == NFC - 1)),
                              inc=inc if fc == NFC - 1 else None)

                def emit_statq(pq):
                    sqv_p, j_p, win_p = pq
                    em.wait_val("t", "ss", sqv_p)
                    em.wait_val("t", "sv", rel.get(("kstatread",), 0))
                    stqv = em.op("t", lambda t:
                                 t.matmul(P[7][0:1, 0:512], ones_r[:, 0:1], qsqbQ[:, :],
                                          start=True, stop=True), inc="st")
                    rel[("stqv",)] = stqv
                    em.wait_val("v", "st", stqv)
                    if win_p == 0:
                        em.op("v", lambda v, j_p=j_p:
                              v.reduce_max(qmax[:, j_p:j_p + 1], P[7][0:1, 0:512], axis=AX.X),
                              inc="sv")
                    else:
                        em.op("v", lambda v: v.reduce_max(kmw[:, :], P[7][0:1, 0:512], axis=AX.X))
                        em.op("v", lambda v, j_p=j_p:
                              v.tensor_tensor(qmax[:, j_p:j_p + 1], qmax[:, j_p:j_p + 1],
                                              kmw[:, :], op=ALU.max), inc="sv")
                    rel[("statfree",)] = em.val("sv")

                for win in range(NWIN):
                    wsl = slice(win * 512, (win + 1) * 512)
                    em.wait_val("t", "sv", rel.get(("kfree",), 0))
                    proj(P[0], 2 * DM, win)
                    proj(P[1], 3 * DM, win, inc="st")
                    em.wait("v", "st")
                    em.op("v", lambda v, dc=dc, wsl=wsl:
                          v.scalar_tensor_tensor(t1b[:, :], P[0][:, 0:512], bkp[:, dc:dc + 1],
                                                 Ct[:, wsl], op0=ALU.add, op1=ALU.mult))
                    em.op("v", lambda v, dc=dc, wsl=wsl:
                          v.scalar_tensor_tensor(t2b[:, :], P[1][:, 0:512], bks[:, dc:dc + 1],
                                                 St[:, wsl], op0=ALU.add, op1=ALU.mult))
                    em.op("v", lambda v, j=j, wsl=wsl:
                          v.tensor_tensor(krotg(j)[:, wsl], t1b[:, :], t2b[:, :], op=ALU.add),
                          inc="sv")
                    rel[("kfree",)] = em.val("sv")
                    em.wait("s", "sv")
                    em.wait_val("s", "st", rel.get(("stkv",), 0))
                    skv = em.op("s", lambda sc, j=j, wsl=wsl:
                                sc.activation(qsqbK[:, :], krotg(j)[:, wsl], AF.Square),
                                inc="ss")
                    if rel.get(("pendq",)) is not None:
                        emit_statq(rel[("pendq",)])
                        rel[("pendq",)] = None
                    em.wait_val("t", "sv", rel.get(("vfree",), 0))
                    proj(P[6], 4 * DM, win, inc="st")
                    pvv = em.val("st")
                    em.wait_val("v", "st", pvv)
                    em.op("v", lambda v, dc=dc:
                          v.tensor_scalar(vtmp[:, :], P[6][:, 0:512], bvn[:, dc:dc + 1], None,
                                          op0=ALU.add), inc="sv")
                    vtmpv = em.val("sv")
                    rel[("vfree",)] = vtmpv
                    em.wait_val("t", "sv", rel.get(("qfree",), 0))
                    proj(P[2], 0, win)
                    proj(P[3], DM, win, inc="st")
                    em.wait("v", "st")
                    em.op("v", lambda v, dc=dc, wsl=wsl:
                          v.scalar_tensor_tensor(t1b[:, :], P[2][:, 0:512], bqp[:, dc:dc + 1],
                                                 Ct[:, wsl], op0=ALU.add, op1=ALU.mult))
                    em.op("v", lambda v, dc=dc, wsl=wsl:
                          v.scalar_tensor_tensor(t2b[:, :], P[3][:, 0:512], bqs[:, dc:dc + 1],
                                                 St[:, wsl], op0=ALU.add, op1=ALU.mult))
                    em.op("v", lambda v: v.tensor_tensor(t1b[:, :], t1b[:, :], t2b[:, :], op=ALU.add))
                    em.op("v", lambda v, j=j, wsl=wsl:
                          v.tensor_scalar(qrotg(j)[:, wsl], t1b[:, :], scl[:, :], None,
                                          op0=ALU.mult), inc="sv")
                    rel[("qfree",)] = em.val("sv")
                    em.wait("s", "sv")
                    em.wait_val("s", "st", rel.get(("stqv",), 0))
                    sqv = em.op("s", lambda sc, j=j, wsl=wsl:
                                sc.activation(qsqbQ[:, :], qrotg(j)[:, wsl], AF.Square),
                                inc="ss")
                    rel[("pendq",)] = (sqv, j, win)
                    em.wait_val("t", "ss", skv)
                    em.wait_val("t", "sv", rel.get(("statfree",), 0))
                    stkv = em.op("t", lambda t:
                                 t.matmul(P[7][0:1, 0:512], ones_r[:, 0:1], qsqbK[:, :],
                                          start=True, stop=True), inc="st")
                    rel[("stkv",)] = stkv
                    em.wait_val("v", "st", stkv)
                    if win == 0:
                        em.op("v", lambda v, j=j:
                              v.reduce_max(kmax[:, j:j + 1], P[7][0:1, 0:512], axis=AX.X),
                              inc="sv")
                    else:
                        em.op("v", lambda v: v.reduce_max(kmw[:, :], P[7][0:1, 0:512], axis=AX.X))
                        em.op("v", lambda v, j=j:
                              v.tensor_tensor(kmax[:, j:j + 1], kmax[:, j:j + 1], kmw[:, :],
                                              op=ALU.max), inc="sv")
                    rel[("kstatread",)] = em.val("sv")
                    em.wait_val("t", "sv", vtmpv)
                    em.wait_val("t", "ss", rel.get(("vtfree",), 0))
                    for h2 in range(2):
                        for jj in range(4):
                            em.op("t", lambda t, h2=h2, jj=jj:
                                  t.transpose(P[4 + h2][:, :].bitcast(BF16)[:, jj * 64:
                                                                            (jj + 1) * 64],
                                              vtmp[h2 * 64:(h2 + 1) * 64, jj * 128:(jj + 1) * 128],
                                              idb[h2 * 64:(h2 + 1) * 64, h2 * 64:(h2 + 1) * 64]),
                                  inc="st" if (h2 == 1 and jj == 3) else None)
                    em.wait("s", "st")
                    for h2 in range(2):
                        for jj in range(4):
                            kcg = win * 4 + jj
                            em.op("s", lambda sc, h2=h2, jj=jj, kcg=kcg, j=j:
                                  sc.activation(vhatg(j)[:, (h2 * 16 + kcg) * 65:
                                                         (h2 * 16 + kcg) * 65 + 64],
                                                P[4 + h2][:, :].bitcast(BF16)[:, jj * 64:
                                                                              (jj + 1) * 64],
                                                AI),
                                  inc="ss" if (h2 == 1 and jj == 3) else None)
                    rel[("vtfree",)] = em.val("ss")
                rel[("vhev", s, dc)] = em.val("ss")
                rel[("wsfree", s, dc % 2)] = em.val("st")
                if dc + 2 < NP:
                    rel[("wp", s, dc + 2)] = em.val("st")
                    issue_pair_bundle(s, dc + 2)
            if phases < 3.5:
                continue
            if rel.get(("pendq",)) is not None:
                emit_statq(rel[("pendq",)])
                rel[("pendq",)] = None
            em.op("v", lambda v: v.tensor_tensor(mh2[:, :], qmax[:, :], kmax[:, :],
                                                 op=ALU.mult), inc="sv")
            em.wait("s", "sv")
            em.op("s", lambda sc: sc.activation(mh2[:, :], mh2[:, :], AF.Sqrt), inc="ss")
            em.wait("v", "ss")
            em.op("v", lambda v: v.tensor_scalar(mh2b[:, :], mh2[:, :], -1.0, None,
                                                 op0=ALU.mult), inc="sv")
            em.wait("t", "sv")
            em.op("t", lambda t: t.matmul(P[7][:, 0:GRP], ones_r[0:1, 0:128], mh2b[0:1, 0:GRP],
                                          start=True, stop=True), inc="st")
            em.wait("v", "st")
            em.op("v", lambda v: v.tensor_copy(negmh[:, :], P[7][:, 0:GRP]), inc="sv")
            nmv = em.val("sv")

            if phases < 4:
                continue
            SA = [S2[0][:, 0:512], S2[1][:, 0:512]]
            SB = [S2[0][:, 512:1024], S2[1][:, 512:1024]]
            em.wait("t", "ss")
            for j, dc in enumerate(dcs):
                em.wait_val("t", "ss", rel[("vhev", s, dc)])
                em.wait_val("t", "sv", nmv)
                for win in range(NWIN):
                    nkc = 4 * win + 4
                    for stage in range(nkc + 1):
                        if stage < nkc:
                            kc = stage
                            kb = kc % 2
                            off = max(0, (kc - 4 * win) * 128)
                            diag = kc >= 4 * win
                            qs0 = win * 512 + off
                            qs1 = (win + 1) * 512
                            em.wait_val("t", "ss", rel.get(("sexp", kb), 0))
                            for (Sp, base) in ((SA[kb], 0), (SB[kb], 64)):
                                lastg = (base == 64) and not diag
                                em.op("t", lambda t, Sp=Sp, base=base, kc=kc, off=off, qs0=qs0,
                                            qs1=qs1, diag=diag, j=j:
                                      t.matmul(Sp[:, off:512],
                                               krotg(j)[base:base + 64, kc * 128:(kc + 1) * 128],
                                               qrotg(j)[base:base + 64, qs0:qs1],
                                               start=True, stop=not diag),
                                      inc="st" if lastg else None)
                                if diag:
                                    em.op("t", lambda t, Sp=Sp, off=off:
                                          t.matmul(Sp[:, off:off + 128], idb[:, :], tri[:, :],
                                                   start=False, stop=True),
                                          inc="st" if base == 64 else None)
                            em.wait("s", "st")
                            em.op("s", lambda sc, kb=kb, off=off, j=j:
                                  sc.activation(pt[kb][:, off:1024], S2[kb][:, off:1024],
                                                AF.Exp, bias=negmh[:, j:j + 1]), inc="ss")
                            rel[("sexp", kb)] = em.val("ss")
                            rel[("expv", j, kc)] = em.val("ss")
                        if stage >= 1:
                            kc = stage - 1
                            kb = kc % 2
                            off = max(0, (kc - 4 * win) * 128)
                            em.wait_val("t", "ss", rel[("expv", j, kc)])
                            em.op("t", lambda t, kb=kb, kc=kc, off=off, j=j,
                                        last=(kc == nkc - 1):
                                  t.matmul(P[6][0:65, off:512],
                                           vhatg(j)[:, kc * 65: kc * 65 + 65],
                                           pt[kb][:, off:512], start=(kc == 0), stop=last))
                            em.op("t", lambda t, kb=kb, kc=kc, off=off, j=j,
                                        last=(kc == nkc - 1):
                                  t.matmul(P[7][0:65, off:512],
                                           vhatg(j)[:, (16 + kc) * 65: (16 + kc) * 65 + 65],
                                           pt[kb][:, 512 + off:1024], start=(kc == 0), stop=last),
                                  inc="st")
                    em.wait("s", "st")
                    em.wait_val("s", shsem[win % 2], 16 * rel.get(("shcnt", win % 2), 0))
                    em.wait_val("s", "ssum", 32 * rel.get(("sumcnt",), 0))
                    em.op("s", lambda sc, win=win:
                          sc.activation(s64[64:65, win * 512:(win + 1) * 512],
                                        P[6][64:65, 0:512], AI))
                    em.op("s", lambda sc, win=win:
                          sc.activation(s64[64:65, 2048 + win * 512: 2048 + (win + 1) * 512],
                                        P[7][64:65, 0:512], AI))
                    em.op("s", lambda sc, dc=dc, win=win:
                          sc.activation(onT[0:64, dc * L + win * 512: dc * L + (win + 1) * 512],
                                        P[6][0:64, 0:512], AI))
                    em.op("s", lambda sc, win=win:
                          sc.activation(obt[win % 2][:, :], P[7][0:64, 0:512], AI), inc="ss")
                    em.wait("g", "ss")
                    em.op("g", lambda g, dc=dc, win=win:
                          g.dma_start(out=onT[64:128, dc * L + win * 512: dc * L + (win + 1) * 512],
                                      in_=obt[win % 2][:, :]), inc=shsem[win % 2], n=16)
                    rel[("shcnt", win % 2)] = rel.get(("shcnt", win % 2), 0) + 1
                em.wait_val("g", "sv", rel.get(("sumfree",), 0))
                em.op("g", lambda g, dc=dc:
                      g.dma_start(out=sums[2 * dc:2 * dc + 1, 0:L], in_=s64[64:65, 0:L]),
                      inc="ssum", n=16)
                em.op("g", lambda g, dc=dc:
                      g.dma_start(out=sums[2 * dc + 1:2 * dc + 2, 0:L], in_=s64[64:65, L:2 * L]),
                      inc="ssum", n=16)
                rel[("sumcnt",)] = rel.get(("sumcnt",), 0) + 1

        if phases < 5:
            continue
        em.barrier()
        em.wait("v", "ssh0")
        em.wait("v", "ssh1")
        em.wait("v", "ssum")
        def _recip(v, nc=nc):
            with nc.allow_low_precision(reason="recip in f32 internally, one rounding"):
                return v.reciprocal(rsbf[:, :], sums[:, :])

        em.op("v", _recip, inc="sv")
        rel[("sumfree",)] = em.val("sv")
        rsbase = [0, 32]
        for dc in range(NP):
            em.wait("y", "sv")
            em.wait_val("y", "st", rel.get(("rsfree", dc % 2), 0))
            rb = rsbase[dc % 2]
            em.op("y", lambda y, dc=dc, rb=rb: y.dma_start(out=s64[rb:rb + 1, 0:L],
                                                           in_=rsbf[2 * dc:2 * dc + 1, 0:L]),
                  inc="srs", n=16)
            rsw = em.op("y", lambda y, dc=dc, rb=rb:
                        y.dma_start(out=s64[rb:rb + 1, L:2 * L],
                                    in_=rsbf[2 * dc + 1:2 * dc + 2, 0:L]),
                        inc="srs", n=16)
            em.wait_val("t", "srs", rsw)
            for win in range(NWIN):
                par = (dc * NWIN + win) % 2
                bank = P[2 + par]
                em.wait_val("t", "sv", rel.get(("obc", par), 0))
                em.op("t", lambda t, win=win, bank=bank, rb=rb:
                      t.matmul(bank[0:64, 0:512], ones_r[rb:rb + 1, 0:64],
                               s64[rb:rb + 1, win * 512:(win + 1) * 512],
                               start=True, stop=True))
                em.op("t", lambda t, win=win, bank=bank, rb=rb:
                      t.matmul(bank[64:128, 0:512], ones_r[rb:rb + 1, 0:64],
                               s64[rb:rb + 1, L + win * 512: L + (win + 1) * 512],
                               start=True, stop=True), inc="st")
                if win == NWIN - 1:
                    rel[("rsfree", dc % 2)] = em.val("st")
                em.wait("v", "st")
                em.op("v", lambda v, dc=dc, win=win, bank=bank:
                      v.tensor_tensor(onT[:, dc * L + win * 512: dc * L + (win + 1) * 512],
                                      onT[:, dc * L + win * 512: dc * L + (win + 1) * 512],
                                      bank[:, 0:512], op=ALU.mult), inc="sv")
                rel[("obc", par)] = em.val("sv")

        if phases < 6:
            continue
        em.barrier()
        for icol in range(2):
            em.wait_val("g", "sst", rel.get(("accst",), 0))
            em.wait("g", "sv")
            for tt in range(NT):
                em.op("g", lambda g, s=s, tt=tt, icol=icol:
                      g.dma_start(out=acc[:, tt * 512:(tt + 1) * 512],
                                  in_=x_d[s, tt * 128:(tt + 1) * 128,
                                          icol * 512:(icol + 1) * 512]),
                      inc="sxf", n=16)
            xe_wait = {}
            st0 = em.val("st")
            for e in range(9):
                slot = e % 2
                if icol == 0 and e < 2:
                    em.wait_val("g", "st", rel.get(("wsfree", s, e), 0))
                else:
                    em.wait_val("g", "st", st0 if e < 2 else st0 + (e - 1) * 16)
                xe_wait[e] = em.op("g", lambda g, e=e, icol=icol, slot=slot:
                                   g.dma_start(out=wslot[slot][:, 0:NFC * 512],
                                               in_=wex_d[e, icol]),
                                   inc=xesem[slot], n=16)
            for e in range(9):
                slot = e % 2
                em.wait_val("t", xesem[slot], xe_wait[e])
                for tt in range(NT):
                    par = tt % 2
                    bank = P[par]
                    em.wait_val("t", "sv", rel.get(("fps", par), 0))
                    src = xnT if e < 8 else onT
                    for fc in range(NFC):
                        em.op("t", lambda t, bank=bank, fc=fc, tt=tt, slot=slot, src=src, e=e:
                              t.matmul(bank[:, 0:512],
                                       src[:, fc * L + tt * 128: fc * L + (tt + 1) * 128],
                                       wslot[slot][:, fc * 512:(fc + 1) * 512],
                                       start=(fc == 0), stop=(e < 8 and fc == NFC - 1)),
                              inc="st" if (e < 8 and fc == NFC - 1) else None)
                    if e == 8:
                        em.op("t", lambda t, bank=bank, icol=icol:
                              t.matmul(bank[:, 0:512], ones_r[0:1, 0:128],
                                       bo_sb[0:1, icol * 512:(icol + 1) * 512],
                                       start=False, stop=False))
                        em.op("t", lambda t, bank=bank, tt=tt, icol=icol:
                              t.matmul(bank[:, 0:512], wT[0:8, tt * 128:(tt + 1) * 128],
                                       eb_sb[0:8, icol * 512:(icol + 1) * 512],
                                       start=False, stop=True), inc="st")
                    em.wait("v", "st")
                    if e == 0:
                        em.wait("v", "sxf")
                        em.wait_val("v", "sst", rel.get(("accst",), 0))
                    asl = acc[:, tt * 512:(tt + 1) * 512]
                    if e < 8:
                        em.op("v", lambda v, bank=bank, tt=tt, e=e, asl=asl:
                              v.scalar_tensor_tensor(asl, bank[:, 0:512],
                                                     wcol[:, tt * 8 + e: tt * 8 + e + 1],
                                                     asl, op0=ALU.mult, op1=ALU.add), inc="sv")
                    else:
                        em.op("v", lambda v, bank=bank, asl=asl:
                              v.tensor_tensor(asl, asl, bank[:, 0:512], op=ALU.add), inc="sv")
                    rel[("fps", par)] = em.val("sv")
                    if e == 8:
                        em.wait("y", "sv")
                        em.op("y", lambda y, s=s, tt=tt, icol=icol:
                              y.dma_start(out=out_d[s, tt * 128:(tt + 1) * 128,
                                                    icol * 512:(icol + 1) * 512],
                                          in_=acc[:, tt * 512:(tt + 1) * 512]),
                              inc="sst", n=16)
                        n_stores += 1
            rel[("accst",)] = em.val("sst")

    em.wait_val("y", "sst", 16 * n_stores)

    with nc.Block() as block:
        @block.tensor
        def _(t):
            for f in em.L["t"]:
                f(t)

        @block.vector
        def _(v):
            for f in em.L["v"]:
                f(v)

        @block.scalar
        def _(sc):
            for f in em.L["s"]:
                f(sc)

        @block.gpsimd
        def _(g):
            for f in em.L["g"]:
                f(g)

        @block.sync
        def _(y):
            for f in em.L["y"]:
                f(y)

    es.close()
    return nc


def _host_inputs(L0_attn_Wq, L0_attn_bq, L0_attn_Wk, L0_attn_bk, L0_attn_Wv,
                 L0_attn_bv, L0_attn_Wo, L0_attn_bo, L0_attn_scale,
                 L0_ffn_gate_w, L0_ffn_gate_b, L0_ffn_expert_w, L0_ffn_expert_b):
    f32 = np.float32
    Wq = np.asarray(L0_attn_Wq, f32)
    Wk = np.asarray(L0_attn_Wk, f32)
    Wv = np.asarray(L0_attn_Wv, f32)
    Wo = np.asarray(L0_attn_Wo, f32)
    bq = np.asarray(L0_attn_bq, f32)
    bk = np.asarray(L0_attn_bk, f32)
    bv = np.asarray(L0_attn_bv, f32)
    bo = np.asarray(L0_attn_bo, f32)
    scale = np.asarray(L0_attn_scale, f32)
    gw = np.asarray(L0_ffn_gate_w, f32)
    gb = np.asarray(L0_ffn_gate_b, f32)
    ew = np.asarray(L0_ffn_expert_w, f32)
    eb = np.asarray(L0_ffn_expert_b, f32)

    pi = np.zeros(DM, np.int64)
    sig = np.zeros(DM, np.int64)
    for n in range(NH):
        b0 = n * HD
        pi[b0:b0 + 32] = b0 + 2 * np.arange(32)
        pi[b0 + 32:b0 + 64] = b0 + 2 * np.arange(32) + 1
        sig[b0 + 2 * np.arange(32)] = b0 + 2 * np.arange(32) + 1
        sig[b0 + 2 * np.arange(32) + 1] = b0 + 2 * np.arange(32)

    def pair_img(WT):
        return np.ascontiguousarray(
            WT.reshape(8, 128, 8, 128).transpose(2, 1, 0, 3).reshape(8, 128, 1024))

    ins = {}
    ins["wqp"] = pair_img(Wq.T[:, pi])
    ins["wqs"] = pair_img(Wq.T[:, sig[pi]])
    ins["wkp"] = pair_img(Wk.T[:, pi])
    ins["wks"] = pair_img(Wk.T[:, sig[pi]])
    ins["wvn"] = pair_img(Wv.T)
    wex = np.zeros((9, 2, 128, NFC * 512), f32)
    for e in range(8):
        wex[e] = ew[e].reshape(8, 128, 2, 512).transpose(2, 1, 0, 3).reshape(2, 128, NFC * 512)
    wex[8] = Wo.T.reshape(8, 128, 2, 512).transpose(2, 1, 0, 3).reshape(2, 128, NFC * 512)
    ins["wex"] = np.ascontiguousarray(wex)
    ins["gw"] = np.ascontiguousarray(gw.reshape(8, 128, 8).transpose(1, 0, 2).reshape(128, 64))
    ins["bqp"] = np.ascontiguousarray(bq[pi].reshape(8, 128).T)
    ins["bqs"] = np.ascontiguousarray(bq[sig[pi]].reshape(8, 128).T)
    ins["bkp"] = np.ascontiguousarray(bk[pi].reshape(8, 128).T)
    ins["bks"] = np.ascontiguousarray(bk[sig[pi]].reshape(8, 128).T)
    ins["bvn"] = np.ascontiguousarray(bv.reshape(8, 128).T)
    ins["bo"] = bo.reshape(1, DM)
    ins["gb"] = gb.reshape(1, 8)
    ins["eb"] = eb
    ins["scl"] = np.repeat(scale.reshape(1, 1), 128, axis=0)

    freqs = 10000.0 ** (-np.arange(0, HD, 2, dtype=np.float64) / HD)
    ang = np.arange(L, dtype=np.float64)[:, None] * freqs[None, :]
    cosT = np.cos(ang).astype(f32).T
    sinT = np.sin(ang).astype(f32).T
    C64 = np.concatenate([cosT, cosT], 0)
    S64 = np.concatenate([-sinT, sinT], 0)
    ins["ct"] = np.ascontiguousarray(np.concatenate([C64, C64], 0))
    ins["stb"] = np.ascontiguousarray(np.concatenate([S64, S64], 0))

    k_i = np.arange(128)
    ins["tri"] = np.where(k_i[None, :] < k_i[:, None], f32(-65536.0), f32(0.0))
    ins["idn"] = np.eye(128, dtype=f32)
    e2c = np.zeros((128, 2), f32)
    e2c[0:64, 0] = 1.0
    e2c[64:128, 1] = 1.0
    ins["e2"] = e2c
    ins["ones"] = np.ones((128, 512), f32)
    vini = np.zeros((128, 2 * 16 * 65), f32)
    vini[:, 64::65] = 1.0
    ins["vini"] = vini
    return ins


def kernel(L0_attn_x, L0_attn_Wq, L0_attn_bq, L0_attn_Wk, L0_attn_bk,
           L0_attn_Wv, L0_attn_bv, L0_attn_Wo, L0_attn_bo, L0_attn_scale,
           L0_ffn_gate_w, L0_ffn_gate_b, L0_ffn_expert_w, L0_ffn_expert_b):
    x = np.ascontiguousarray(np.asarray(L0_attn_x, np.float32))
    shared = _host_inputs(L0_attn_Wq, L0_attn_bq, L0_attn_Wk, L0_attn_bk,
                          L0_attn_Wv, L0_attn_bv, L0_attn_Wo, L0_attn_bo,
                          L0_attn_scale, L0_ffn_gate_w, L0_ffn_gate_b,
                          L0_ffn_expert_w, L0_ffn_expert_b)
    nc = build_program()
    in_maps = []
    for c in range(8):
        m = dict(shared)
        m["x"] = x[2 * c:2 * c + 2]
        in_maps.append(m)
    res = run_bass_kernel_spmd(nc, in_maps, core_ids=list(range(8)))
    out = np.concatenate([res.results[c]["out"] for c in range(8)], axis=0)
    return out
